# version 06
# baseline (speedup 1.0000x reference)
import math
from contextlib import ExitStack

import numpy as np
import concourse.bass as bass
import concourse.mybir as mybir

F32 = mybir.dt.float32
BF16 = mybir.dt.bfloat16
I32 = mybir.dt.int32
A = mybir.AluOpType
AF = mybir.ActivationFunctionType
AX = mybir.AxisListType.X

L, D, DIN, NH, HD, NST = 2048, 2048, 4096, 64, 64, 64
T = 512
NT = L // T
NL0 = T // 128
RMS_EPS = 1.1920929e-07
LN_EPS = 1e-5
TWO_PI = 2.0 * math.pi
PI_LO = 3.1415925
NPROJ = 8480

R_SM, R_XC, R_Z = 0, 288, 288 + 4096
FC = [(R_SM, 288, "sm")] + [(R_XC + i * 512, 512, "xc") for i in range(8)] + \
     [(R_Z + i * 512, 512, "z") for i in range(8)]
S_B, S_C, S_DT, S_TH, S_LAM = 0, 64, 128, 192, 224


def ap_of(t, base_part, npart, coloff, dims):
    row = int(np.prod(t.shape[1:]))
    return bass.AP(t, base_part * row + coloff,
                   [[row, npart]] + [[s, c] for (s, c) in dims])


class Sched:
    ENGINES = ("sync", "gpsimd", "vector", "scalar", "tensor")
    DQ = ("sync", "gpsimd", "scalar")

    def __init__(self, nc, es):
        self.nc = nc
        self.es = es
        self.ops = {e: [] for e in self.ENGINES}
        self.sem = {e: es.enter_context(nc.semaphore(f"s_{e}")) for e in self.ENGINES}
        self.cnt = {e: 0 for e in self.ENGINES}
        self.chsem = {}
        self.chcnt = {}
        self.last_w = {}
        self.readers = {}

    def _channel(self, res):
        parts = res.split("_")
        if len(parts) > 1 and all(p.isdigit() for p in parts[1:]):
            return parts[0]
        return res

    def _get_ch(self, ch):
        if ch not in self.chsem:
            self.chsem[ch] = self.es.enter_context(self.nc.semaphore(f"ch_{ch}"))
            self.chcnt[ch] = 0
        return self.chsem[ch]

    def _dep(self, reads, writes):
        toks = []
        for r in reads:
            if r in self.last_w:
                toks.append(self.last_w[r])
        for r in writes:
            if r in self.last_w:
                toks.append(self.last_w[r])
            toks.extend(self.readers.get(r, ()))
        out = []
        seen_ch = set()
        for t in toks:
            if t[0] == "D":
                ch = t[1]
                if ch not in seen_ch:
                    seen_ch.add(ch)
                    out.append(("D", ch, self.chcnt[ch]))
            else:
                out.append(t)
        return out

    def _post(self, tok, reads, writes):
        for r in writes:
            self.last_w[r] = tok
            self.readers[r] = []
        for r in reads:
            self.readers.setdefault(r, []).append(tok)

    def op(self, eng, fn, reads=(), writes=(), extra=()):
        deps = self._dep(tuple(reads) + tuple(x for x in ()), writes)
        for t in extra:
            if t[0] == "D":
                deps.append(("D", t[1], self.chcnt[t[1]]))
            else:
                deps.append(t)
        self.cnt[eng] += 1
        tok = ("c", eng, self.cnt[eng])
        self.ops[eng].append((None, fn, deps, tuple(reads), tuple(writes)))
        self._post(tok, reads, writes)
        return tok

    def dma(self, eng, fn, reads=(), writes=(), extra=()):
        assert eng in self.DQ and writes
        ch = self._channel(writes[0])
        self._get_ch(ch)
        deps = self._dep(reads, writes)
        for t in extra:
            if t[0] == "D":
                deps.append(("D", t[1], self.chcnt[t[1]]))
            else:
                deps.append(t)
        self.chcnt[ch] += 16
        tok = ("D", ch, self.chcnt[ch])
        self.ops[eng].append((ch, fn, deps, tuple(reads), tuple(writes)))
        self._post(tok, reads, writes)
        return tok

    def emit(self, block):
        sched = self

        def mk(e_name):
            def body(eng):
                waited = {}
                dirty = set()
                dirty_r = set()
                for ch, fn, deps, rds, wrs in sched.ops[e_name]:
                    for dep in deps:
                        k, src, val = dep
                        if k == "c" and src == e_name:
                            continue
                        key = (k, src)
                        if waited.get(key, -1) >= val:
                            continue
                        waited[key] = val
                        eng.wait_ge(sched.sem[src] if k == "c" else sched.chsem[src], val)
                    if e_name in ("vector", "scalar", "gpsimd"):
                        if (any(r in dirty for r in rds) or any(w in dirty for w in wrs)
                                or any(w in dirty_r for w in wrs)):
                            eng.drain()
                            dirty.clear()
                            dirty_r.clear()
                    inst = fn(eng)
                    if ch is not None:
                        if e_name in ("vector", "scalar", "gpsimd"):
                            dirty.update(wrs)
                            dirty_r.update(rds)
                        inst.then_inc(sched.chsem[ch], 16)
                    else:
                        if e_name in ("vector", "scalar", "gpsimd"):
                            dirty.update(wrs)
                            dirty_r.update(rds)
                        inst.then_inc(sched.sem[e_name], 1)
            return body

        block.sync(mk("sync"))
        block.gpsimd(mk("gpsimd"))
        block.vector(mk("vector"))
        block.scalar(mk("scalar"))
        block.tensor(mk("tensor"))


def build(dbg=(), ssd_upto=99, dump_scr=False):
    nc = bass.Bass("TRN2", target_bir_lowering=False)
    es = ExitStack()
    dbg = set(dbg)

    x_in = nc.dram_tensor("x", [L, D], F32, kind="ExternalInput")
    w_in = nc.dram_tensor("w_in", [8384, D], F32, kind="ExternalInput")
    w_th = nc.dram_tensor("w_theta", [32, D], F32, kind="ExternalInput")
    w_lm = nc.dram_tensor("w_lambda", [64, D], F32, kind="ExternalInput")
    w_out = nc.dram_tensor("w_out", [D, DIN], F32, kind="ExternalInput")
    b_bias = nc.dram_tensor("b_bias", [1, NST], F32, kind="ExternalInput")
    c_bias = nc.dram_tensor("c_bias", [1, NST], F32, kind="ExternalInput")
    a_log = nc.dram_tensor("a_log", [1, NH], F32, kind="ExternalInput")
    d_par = nc.dram_tensor("d_param", [1, NH], F32, kind="ExternalInput")
    dt_bias_d = nc.dram_tensor("dt_bias", [1, NH], F32, kind="ExternalInput")
    fw = nc.dram_tensor("ffn_w", [D, D], F32, kind="ExternalInput")
    fb = nc.dram_tensor("ffn_b", [1, D], F32, kind="ExternalInput")
    y_out = nc.dram_tensor("y", [L, D], F32, kind="ExternalOutput")

    KS = "ExternalOutput" if dump_scr else "Internal"
    wq_in = nc.dram_tensor("wq_in", [NPROJ, D], BF16, kind=KS)
    wq_out = nc.dram_tensor("wq_out", [D, DIN], BF16, kind=KS)
    wq_ffn = nc.dram_tensor("wq_ffn", [D, D], BF16, kind=KS)
    xn_scr = nc.dram_tensor("xn_scr", [L, D], BF16, kind=KS)
    xh_scr = nc.dram_tensor("xh_scr", [L + 1, DIN], BF16, kind=KS)
    sz_scr = nc.dram_tensor("sz_scr", [L, DIN], BF16, kind=KS)
    yg_scr = nc.dram_tensor("yg_scr", [L, DIN], BF16, kind=KS)
    xq_scr = nc.dram_tensor("xq_scr", [L, D], BF16, kind=KS)
    x2_scr = nc.dram_tensor("x2_scr", [L, D], F32, kind=KS)
    bounce = nc.dram_tensor("bounce", [16, 4096], F32, kind="Internal")

    taps = {}
    def tap(name, shape, dt):
        if name in dbg:
            taps[name] = nc.dram_tensor("tap_" + name, shape, dt, kind="ExternalOutput")
        return taps.get(name)

    tap("smalls", [T, 288], F32)
    tap("acs", [128, NH], F32)
    tap("X", [128, DIN], BF16)
    tap("bd", [128, NH * NST], BF16)
    tap("M", [128, 32 * 128], BF16)
    tap("sst", [128, 2048], F32)
    tap("trig", [128, 64], F32)
    tap("attnb", [128, D], F32)
    tap("istd4", [128, 8], F32)

    sbT = lambda name, shape, dt: es.enter_context(nc.sbuf_tensor(name, shape, dt))
    psT = lambda name, shape, dt: es.enter_context(nc.psum_tensor(name, shape, dt))

    ident_bf = sbT("ident_bf", [128, 128], BF16)
    ident_f = sbT("ident_f", [128, 128], F32)
    u2 = sbT("u2", [128, 128], F32)
    u64 = sbT("u64", [64, 64], F32)
    dtb_bc = sbT("dtb_bc", [128, NH], F32)
    bb_bc = sbT("bb_bc", [128, NST], F32)
    cb_bc = sbT("cb_bc", [128, NST], F32)
    negae_bc = sbT("negae_bc", [128, NH], F32)
    dpar_bc = sbT("dpar_bc", [128, NH], BF16)
    ws_bc = sbT("ws_bc", [128, D], F32)
    bias_bc = sbT("bias_bc", [128, D], F32)
    arow = sbT("arow", [1, NH], F32)
    arow2 = sbT("arow2", [1, NH], F32)
    cst = sbT("cst", [128, 4], F32)
    qnat = sbT("qnat", [128, D], F32)
    qu = sbT("qu", [128, D], F32)
    qk = sbT("qk", [128, D], I32)
    qfq = sbT("qfq", [128, D], BF16)
    qgm = sbT("qgm", [128, 32], F32)
    qs = sbT("qs", [128, 32], F32)
    qrs = sbT("qrs", [128, 32], F32)
    gnat = sbT("gnat", [128, 1024], F32)
    gu2 = sbT("gu2", [128, 1024], F32)
    gk = sbT("gk", [128, 1024], I32)
    gfq = sbT("gfq", [128, 1024], BF16)
    ggm = sbT("ggm", [128, 16], F32)
    gs2 = sbT("gs2", [128, 16], F32)
    grs = sbT("grs", [128, 16], F32)
    xnT = sbT("xnT", [128, 16 * T], BF16)
    wslab = [sbT("wslab0", [128, 8 * 512], BF16), sbT("wslab1", [128, 8 * 512], BF16)]
    xl0 = sbT("xl0", [128, D], F32)
    scr2 = sbT("scr2", [128, D], F32)
    scr3 = qnat
    dump = sbT("dump", [128, D], F32)
    stage = [sbT("stage0", [128, 512], BF16), sbT("stage1", [128, 512], BF16)]
    smalls = sbT("smalls", [128, NL0 * 288], F32)
    xh_l0 = sbT("xh_l0", [128, DIN], BF16)
    xsh_l0 = sbT("xsh_l0", [128, DIN], BF16)
    X_l0 = sbT("X_l0", [128, DIN], BF16)
    xd2 = [sbT("xd0", [128, 512], BF16), sbT("xd1", [128, 512], BF16)]
    dt_t = sbT("dt_t", [128, NH], F32)
    adisc = sbT("adisc", [128, NH], F32)
    gam_bf = sbT("gam_bf", [128, NH], BF16)
    bet_bf = sbT("bet_bf", [128, NH], BF16)
    lam_t = sbT("lam_t", [128, NH], F32)
    w64a = sbT("w64a", [128, NH], F32)
    w64b = sbT("w64b", [128, NH], F32)
    w64c = sbT("w64c", [128, NH], F32)
    bn_bf = sbT("bn_bf", [128, NST], BF16)
    cn_bf = sbT("cn_bf", [128, NST], BF16)
    th_t = sbT("th_t", [128, 32], F32)
    angles = sbT("angles", [128, 32], F32)
    sin_t = sbT("sin_t", [128, 32], F32)
    cos_t = sbT("cos_t", [128, 32], F32)
    tg_u = sbT("tg_u", [128, 32], F32)
    tg_k = sbT("tg_k", [128, 32], I32)
    tg_c = sbT("tg_c", [128, 32], F32)
    offT = sbT("offT", [32, 1], F32)
    thT = sbT("thT", [32, 128], F32)
    zeros32 = sbT("zeros32", [32, 128], F32)
    angT = sbT("angT", [32, 128], F32)
    acs = sbT("acs", [128, NH], F32)
    acsT = sbT("acsT", [64, 128], F32)
    ea_t = sbT("ea_t", [128, NH], F32)
    dectmp = sbT("dectmp", [64, 128], F32)
    dect = sbT("dect", [64, 128], BF16)
    dec_tm = sbT("dec_tm", [128, NH], BF16)
    decay_arr = sbT("decay_arr", [128, 64], F32)
    bnT = sbT("bnT", [64, 128], BF16)
    cnT_e = sbT("cnT_e", [128, 128], BF16)
    cnT_o = sbT("cnT_o", [128, 128], BF16)
    gu_bd = sbT("gu_bd", [128, 128], BF16)
    bd2 = sbT("bd2", [128, 2 * NH * NST], BF16)
    dsrc = sbT("dsrc", [128, 1024], F32)
    lpair = sbT("lpair", [128, 16 * 128], BF16)
    mbd = lpair
    s_st = sbT("s_st", [128, 2048], F32)
    s_bf2 = [sbT("s_bf0", [128, 256], BF16), sbT("s_bf1", [128, 256], BF16)]
    eab2 = [sbT("eab0", [128, 512], BF16), sbT("eab1", [128, 512], BF16)]
    ssq8 = sbT("ssq8", [128, NL0 * 8], F32)
    istd_t = sbT("istd_t", [128, NL0], F32)
    xs_t = sbT("xs_t", [128, NL0], F32)
    mean_t = sbT("mean_t", [128, 1], F32)
    m2_t = sbT("m2_t", [128, 1], F32)
    var_t = sbT("var_t", [128, 1], F32)
    ssq_t = sbT("ssq_t", [128, 1], F32)
    wgt = [sbT("wgt0", [128, D], BF16), sbT("wgt1", [128, D], BF16)]
    lhsb = [sbT("lhs0", [128, T], BF16), sbT("lhs1", [128, T], BF16)]

    P = [psT(f"p{i}", [128, 512], F32) for i in range(7)]
    psm = psT("psm", [128, 512], F32)
    ptr_bf = psm[:, 384:512].bitcast(BF16)

    sch = Sched(nc, es)
    OP, DMA = sch.op, sch.dma
    _mmb = [0]

    def npsw(e, base):
        if _mmb[0] != base:
            e.nop()
            _mmb[0] = base

    def ident_fill(t):
        OP("gpsimd", lambda e, t=t: e.memset(t[:], 0.0), writes=(t.name,))
        OP("gpsimd", lambda e, t=t: e.affine_select(out=t[:], in_=t[:],
           compare_op=A.not_equal, fill=1.0, base=0, pattern=[[-1, t.shape[1]]],
           channel_multiplier=1), reads=(t.name,), writes=(t.name,))

    ident_fill(ident_bf)
    ident_fill(ident_f)
    OP("gpsimd", lambda e: e.memset(cst[:, 0:1], LN_EPS), writes=("cst",))
    OP("gpsimd", lambda e: e.memset(cst[:, 1:2], RMS_EPS), reads=("cst",), writes=("cst",))
    OP("gpsimd", lambda e: e.memset(cst[:, 2:3], 1.0), reads=("cst",), writes=("cst",))
    OP("gpsimd", lambda e: e.memset(u2[:], 0.0), writes=("u2",))
    OP("gpsimd", lambda e: e.affine_select(out=u2[:], in_=u2[:], compare_op=A.is_gt,
       fill=1.0, base=0, pattern=[[-1, 128]], channel_multiplier=1),
       reads=("u2",), writes=("u2",))
    OP("gpsimd", lambda e: e.memset(u2[0:64, 64:128], 0.0), reads=("u2",), writes=("u2",))
    OP("gpsimd", lambda e: e.memset(u64[:], 0.0), writes=("u64",))
    OP("gpsimd", lambda e: e.affine_select(out=u64[:], in_=u64[:], compare_op=A.is_gt,
       fill=1.0, base=0, pattern=[[-1, 64]], channel_multiplier=1),
       reads=("u64",), writes=("u64",))
    OP("gpsimd", lambda e: e.memset(gu_bd[:], 0.0), writes=("gu_bd",))
    OP("gpsimd", lambda e: e.memset(lpair[:], 0.0), writes=("lpair",))
    OP("gpsimd", lambda e: e.memset(bd2[:], 0.0), writes=("bd2",))
    OP("gpsimd", lambda e: e.memset(cnT_e[:], 0.0), writes=("cnT_e",))
    OP("gpsimd", lambda e: e.memset(cnT_o[:], 0.0), writes=("cnT_o",))
    OP("gpsimd", lambda e: e.memset(zeros32[:], 0.0), writes=("zeros32",))
    OP("gpsimd", lambda e: e.memset(offT[:], 0.0), writes=("offT",))
    OP("gpsimd", lambda e: e.memset(s_st[:], 0.0), writes=tuple(f"s_st_{o}" for o in range(8)))
    OP("gpsimd", lambda e: e.memset(xsh_l0[:], 0.0), writes=("xsh_l0",))
    for _zc in range(8):
        DMA("gpsimd", lambda e, _zc=_zc: e.dma_start(xh_scr[0:1, _zc * 512:(_zc + 1) * 512],
            xsh_l0[0:1, 0:512]), reads=("xsh_l0",), writes=("xh_0_0",))

    def bc_load(dst_t, src_t, w):
        DMA("sync", lambda e, d=dst_t, s=src_t, w=w: e.dma_start(
            d[:, 0:w], bass.AP(s, 0, [[0, 128], [1, w]])), writes=(dst_t.name,))

    bc_load(dtb_bc, dt_bias_d, NH)
    bc_load(bb_bc, b_bias, NST)
    bc_load(cb_bc, c_bias, NST)
    bc_load(bias_bc, fb, D)
    DMA("sync", lambda e: e.dma_start(w64a[:, 0:NH],
        bass.AP(d_par, 0, [[0, 128], [1, NH]])), writes=("w64a",))
    OP("vector", lambda e: e.tensor_copy(dpar_bc[:], w64a[:, 0:NH]),
       reads=("w64a",), writes=("dpar_bc",))
    DMA("sync", lambda e: e.dma_start(arow[:], a_log[0:1, :]), writes=("arow",))
    OP("scalar", lambda e: e.activation(arow2[:], arow[:], AF.Exp, scale=1.0),
       reads=("arow",), writes=("arow2",))
    OP("vector", lambda e: e.tensor_scalar_mul(arow2[:], arow2[:], -1.0),
       reads=("arow2",), writes=("arow2",))
    DMA("gpsimd", lambda e: e.dma_start(bounce[0:1, 0:NH], arow2[:]),
        reads=("arow2",), writes=("bounce0",))
    DMA("sync", lambda e: e.dma_start(negae_bc[:],
        bass.AP(bounce, 0, [[0, 128], [1, NH]])), reads=("bounce0",), writes=("negae_bc",))

    QBUF = {"vector": (qnat, qu, qk, qfq, qgm, qs, qrs),
            "gpsimd": (gnat, gu2, gk, gfq, ggm, gs2, grs)}

    def quant_grouped(src_dmas, dst_t, dst_row, Pn, W, ngrp, gsz, qmax, res_out, dst_col=0, ve="vector"):
        qnat, qu, qk, qfq, qgm, qs, qrs = QBUF[ve]
        ve = "vector"
        off = float(qmax + 1)
        for fn in src_dmas:
            DMA("sync", fn, writes=(qnat.name,))
        OP("vector", lambda e, Pn=Pn, W=W, ng=ngrp:
           e.reduce_max(qgm[0:Pn, 0:ng].rearrange("p (g o) -> p g o", o=1),
                        qnat[0:Pn, 0:W].rearrange("p (g c) -> p g c", g=ng),
                        axis=AX, apply_absolute_value=True),
           reads=(qnat.name,), writes=(qgm.name,))
        OP(ve, lambda e, Pn=Pn, ng=ngrp, q=qmax, fl=(1e-10 if qmax == 7 else 1e-8):
           e.tensor_scalar(qs[0:Pn, 0:ng], qgm[0:Pn, 0:ng], fl, 1.0 / q, A.max, A.mult),
           reads=(qgm.name,), writes=(qs.name,))
        OP("vector", lambda e, Pn=Pn, ng=ngrp: e.reciprocal(qrs[0:Pn, 0:ng], qs[0:Pn, 0:ng]),
           reads=(qs.name,), writes=(qrs.name,))
        for g in range(ngrp):
            OP(ve, lambda e, Pn=Pn, g=g, gsz=gsz, off=off:
               e.tensor_scalar(qu[0:Pn, g * gsz:(g + 1) * gsz],
                               qnat[0:Pn, g * gsz:(g + 1) * gsz],
                               qrs[0:Pn, g:g + 1], off + 0.5, A.mult, A.add),
               reads=(qnat.name, qrs.name), writes=(qu.name,))
        OP(ve, lambda e, Pn=Pn, W=W: e.tensor_scalar(qk[0:Pn, 0:W], qu[0:Pn, 0:W],
           -0.5, None, A.add, A.bypass), reads=(qu.name,), writes=(qk.name,))
        OP(ve, lambda e, Pn=Pn, W=W: e.scalar_tensor_tensor(qu[0:Pn, 0:W],
           qu[0:Pn, 0:W], -1.0, qk[0:Pn, 0:W], A.add, A.is_ge),
           reads=(qu.name, qk.name), writes=(qu.name,))
        OP(ve, lambda e, Pn=Pn, W=W: e.tensor_tensor(qu[0:Pn, 0:W],
           qu[0:Pn, 0:W], qk[0:Pn, 0:W], A.add), reads=(qu.name, qk.name), writes=(qu.name,))
        OP(ve, lambda e, Pn=Pn, W=W, off=off: e.tensor_scalar(qu[0:Pn, 0:W],
           qu[0:Pn, 0:W], -off, None, A.add, A.bypass), reads=(qu.name,), writes=(qu.name,))
        OP(ve, lambda e, Pn=Pn, W=W, ng=ngrp, gsz=gsz:
           e.tensor_tensor(qfq[0:Pn, 0:W].rearrange("p (g c) -> p g c", g=ng),
                           qu[0:Pn, 0:W].rearrange("p (g c) -> p g c", g=ng),
                           ap_of(qs, 0, Pn, 0, [(1, ngrp), (0, gsz)]), A.mult),
           reads=(qu.name, qs.name), writes=(qfq.name,))
        return DMA("gpsimd", lambda e, r=dst_row, Pn=Pn, W=W, d=dst_t, dc=dst_col:
                   e.dma_start(d[r:r + Pn, dc:dc + W], qfq[0:Pn, 0:W]),
                   reads=(qfq.name,), writes=(res_out,))

    def gen_quant_win(jobs, toks):
        for (r, n, srcs) in jobs:
            dmas = []
            for (st, s0, d0, cnt) in srcs:
                dmas.append(lambda e, st=st, a=s0, ro=d0, c=cnt:
                            e.dma_start(qnat[ro:ro + c, 0:D], st[a:a + c, 0:D]))
            tok = quant_grouped(dmas, wq_in, r, n, D, 16, 128, 7, f"wqin_{r}")
            toks["wq_in"].append((r, n, tok))

    def win_jobs(lo, hi):
        segs = [(w_in, 8192, R_SM, 192), (w_th, 0, R_SM + 192, 32),
                (w_lm, 0, R_SM + 224, 64), (w_in, 4096, R_XC, 4096), (w_in, 0, R_Z, 4096)]
        jobs = []
        r = lo
        while r < hi:
            n = min(128, hi - r)
            srcs = []
            for (st, s0, d0, cnt) in segs:
                a, b = max(r, d0), min(r + n, d0 + cnt)
                if a < b:
                    srcs.append((st, s0 + a - d0, a - r, b - a))
            jobs.append((r, n, srcs))
            r += n
        return jobs

    def gen_quant_wout(toks):
        for j in range(16):
            for hf in range(2):
                tok = quant_grouped(
                    [lambda e, j=j, hf=hf: e.dma_start(qnat[0:128, 0:D],
                        w_out[j * 128:(j + 1) * 128, hf * D:(hf + 1) * D])],
                    wq_out, j * 128, 128, D, 16, 128, 7, f"wqout_{j}_{hf}",
                    dst_col=hf * D, ve="vector")
                toks["wq_out"].append((j * 128, 128, tok))

    def gen_quant_ffn(toks):
        for j in range(16):
            for hf in range(2):
                DMA("sync", lambda e, j=j, hf=hf: e.dma_start(gnat[0:128, 0:1024],
                    fw[j * 128:(j + 1) * 128, hf * 1024:(hf + 1) * 1024]), writes=("gnat",))
                if hf == 0:
                    OP("vector", lambda e: e.reduce_max(ggm[:, 0:1], gnat[:, 0:1024], axis=AX,
                       apply_absolute_value=True), reads=("gnat",), writes=("ggm",))
                else:
                    OP("vector", lambda e: e.reduce_max(ggm[:, 1:2], gnat[:, 0:1024], axis=AX,
                       apply_absolute_value=True), reads=("gnat",), writes=("ggm",))
            OP("vector", lambda e: e.tensor_tensor(ggm[:, 0:1], ggm[:, 0:1], ggm[:, 1:2],
               A.max), reads=("ggm",), writes=("ggm",))
            OP("vector", lambda e: e.tensor_scalar(gs2[:, 0:1], ggm[:, 0:1], 1e-8,
               1.0 / 127.0, A.max, A.mult), reads=("ggm",), writes=("gs2",))
            DMA("gpsimd", lambda e, j=j: e.dma_start(bounce[1:2, j * 128:(j + 1) * 128],
                gs2[:, 0:1]), reads=("gs2",), writes=(f"wsrow_{j}",))
            OP("vector", lambda e: e.reciprocal(grs[:, 0:1], gs2[:, 0:1]),
               reads=("gs2",), writes=("grs",))
            for hf in range(2):
                DMA("sync", lambda e, j=j, hf=hf: e.dma_start(gnat[0:128, 0:1024],
                    fw[j * 128:(j + 1) * 128, hf * 1024:(hf + 1) * 1024]), writes=("gnat",))
                OP("vector", lambda e: e.tensor_scalar(gu2[:, 0:1024], gnat[:, 0:1024],
                   grs[:, 0:1], 128.5, A.mult, A.add), reads=("gnat", "grs"), writes=("gu2",))
                OP("vector", lambda e: e.tensor_scalar(gk[:, 0:1024], gu2[:, 0:1024], -0.5,
                   None, A.add, A.bypass), reads=("gu2",), writes=("gk",))
                OP("vector", lambda e: e.scalar_tensor_tensor(gu2[:, 0:1024], gu2[:, 0:1024],
                   -1.0, gk[:, 0:1024], A.add, A.is_ge), reads=("gu2", "gk"), writes=("gu2",))
                OP("vector", lambda e: e.tensor_tensor(gu2[:, 0:1024], gu2[:, 0:1024],
                   gk[:, 0:1024], A.add), reads=("gu2", "gk"), writes=("gu2",))
                OP("vector", lambda e: e.tensor_scalar(gfq[:, 0:1024], gu2[:, 0:1024],
                   -128.0, None, A.add, A.bypass), reads=("gu2",), writes=("gfq",))
                tok = DMA("gpsimd", lambda e, j=j, hf=hf: e.dma_start(
                    wq_ffn[j * 128:(j + 1) * 128, hf * 1024:(hf + 1) * 1024], gfq[:, 0:1024]),
                    reads=("gfq",), writes=(f"wqffn_{j}",))
            toks["wq_ffn"].append((j * 128, 128, tok))
        DMA("sync", lambda e: e.dma_start(ws_bc[:],
            bass.AP(bounce, 4096, [[0, 128], [1, D]])), writes=("ws_bc",),
            extra=[t for (_, _, t) in toks["wq_ffn"]]
                  + [sch.last_w[f"wsrow_{j}"] for j in range(16)])

    def rows_dep(toks, lo, hi):
        return [t for (r, n, t) in toks if r < hi and r + n > lo]

    def istd_pipe(v_res, v_ap, o_ap, o_res, eps_col, scale):
        OP("scalar", lambda e: e.activation(o_ap, v_ap, AF.Ln,
           bias=cst[:, eps_col:eps_col + 1], scale=scale),
           reads=(v_res, "cst"), writes=(o_res,))
        OP("scalar", lambda e: e.activation(o_ap, o_ap, AF.Exp, scale=-0.5),
           reads=(o_res,), writes=(o_res,))

    def ln_stats(src_ap, src_res, extra_reads=()):
        OP("vector", lambda e: e.reduce_sum(mean_t[:], src_ap, axis=AX),
           reads=(src_res,) + tuple(extra_reads), writes=("mean_t",))
        OP("vector", lambda e: e.tensor_scalar_mul(mean_t[:], mean_t[:], 1.0 / D),
           reads=("mean_t",), writes=("mean_t",))
        OP("scalar", lambda e: e.activation(dump[:, 0:D], src_ap, AF.Square,
           accum_out=ssq_t[:]), reads=(src_res,), writes=("dump", "ssq_t"))
        OP("vector", lambda e: e.tensor_tensor(m2_t[:], mean_t[:], mean_t[:], A.mult),
           reads=("mean_t",), writes=("m2_t",))
        OP("vector", lambda e: e.scalar_tensor_tensor(m2_t[:], ssq_t[:], 1.0 / D, m2_t[:],
           A.mult, A.subtract), reads=("ssq_t", "m2_t"), writes=("m2_t",))
        istd_pipe("m2_t", m2_t[:], var_t[:], "var_t", 0, 1.0)

    def gen_ln_x(t):
        for p in range(NL0):
            g0 = t * T + p * 128
            DMA("sync", lambda e, g0=g0: e.dma_start(xl0[:], x_in[g0:g0 + 128, :]),
                writes=("xl0",))
            ln_stats(xl0[:, 0:D], "xl0")
            for cq in range(4):
                st = stage[cq % 2]
                OP("vector", lambda e, cq=cq, st=st: e.tensor_scalar(st[:, :],
                   xl0[:, cq * 512:(cq + 1) * 512], mean_t[:], var_t[:],
                   A.subtract, A.mult), reads=("xl0", "mean_t", "var_t"),
                   writes=(st.name,))
                DMA("gpsimd", lambda e, g0=g0, cq=cq, st=st: e.dma_start(
                    xn_scr[g0:g0 + 128, cq * 512:(cq + 1) * 512], st[:, :]),
                    reads=(st.name,), writes=(f"xn_{t}_{p}",))

    def gen_xnT(t):
        for k in range(16):
            DMA("sync", lambda e, k=k, t=t: e.dma_start_transpose(
                xnT[:, k * T:(k + 1) * T],
                xn_scr[t * T:(t + 1) * T, k * 128:(k + 1) * 128]),
                writes=("xnT",),
                extra=[sch.last_w[f"xn_{t}_{p}"] for p in range(NL0)])

    def gen_inproj(t, qtoks, fcs=None):
        for fc, (rowoff, W, kindname) in enumerate(FC):
            if fcs is not None and fc not in fcs:
                continue
            for kb in range(2):
                slab = wslab[kb]
                for k in range(8):
                    DMA("sync", lambda e, k=k, kb=kb, ro=rowoff, W=W, sl=slab: e.dma_start_transpose(
                        sl[:, k * 512:k * 512 + W],
                        wq_in[ro:ro + W, (kb * 8 + k) * 128:(kb * 8 + k + 1) * 128]),
                        writes=(slab.name,),
                        extra=rows_dep(qtoks["wq_in"], rowoff, rowoff + W))
            for p in range(NL0):
                pa = P[6] if (fc * NL0 + p) % 2 == 0 else psm
                def mmf(e, pa=pa, W=W, p=p):
                    inst = None
                    npsw(e, 0)
                    for kk in range(16):
                        kb, k = kk // 8, kk % 8
                        inst = e.matmul(pa[:, 0:W],
                                        xnT[:, kk * T + p * 128:kk * T + (p + 1) * 128],
                                        wslab[kb][:, k * 512:k * 512 + W],
                                        start=(kk == 0), stop=(kk == 15))
                    return inst
                OP("tensor", mmf, reads=("xnT", "wslab0", "wslab1"), writes=(pa.name,))
                if kindname == "sm":
                    OP("vector", lambda e, pa=pa, p=p: e.tensor_copy(
                       smalls[:, p * 288:(p + 1) * 288], pa[:, 0:288]),
                       reads=(pa.name,), writes=(f"smalls_{p}",))
                else:
                    st = stage[(fc + p) % 2]
                    OP("scalar", lambda e, pa=pa, st=st: e.activation(st[:, :],
                       pa[:, 0:512], AF.Silu), reads=(pa.name,), writes=(st.name,))
                    dst = xh_scr if kindname == "xc" else sz_scr
                    coff = (fc - 1) * 512 if kindname == "xc" else (fc - 9) * 512
                    res = f"xh_{t}_{p}" if kindname == "xc" else f"sz_{t}_{p}"
                    DMA("gpsimd", lambda e, g0=t * T + p * 128, co=coff, st=st, d=dst,
                        ro=(1 if kindname == "xc" else 0):
                        e.dma_start(d[g0 + ro:g0 + ro + 128, co:co + 512], st[:, :]),
                        reads=(st.name,), writes=(res,))

    def gen_smalls(t, p):
        so = p * 288
        smr = f"smalls_{p}"
        OP("vector", lambda e: e.tensor_tensor(w64a[:], smalls[:, so + S_DT:so + S_DT + 64],
           dtb_bc[:], A.add), reads=(smr, "dtb_bc"), writes=("w64a",))
        OP("vector", lambda e: e.scalar_tensor_tensor(w64b[:], w64a[:], -1.0, w64a[:],
           A.mult, A.min), reads=("w64a",), writes=("w64b",))
        OP("scalar", lambda e: e.activation(w64b[:], w64b[:], AF.Exp),
           reads=("w64b",), writes=("w64b",))
        OP("scalar", lambda e: e.activation(w64b[:], w64b[:], AF.Ln,
           bias=cst[:, 2:3]), reads=("w64b", "cst"), writes=("w64b",))
        OP("vector", lambda e: e.scalar_tensor_tensor(dt_t[:], w64a[:], 0.0, w64b[:],
           A.max, A.add), reads=("w64a", "w64b"), writes=("dt_t",))
        OP("vector", lambda e: e.tensor_tensor(adisc[:], dt_t[:], negae_bc[:], A.mult),
           reads=("dt_t", "negae_bc"), writes=("adisc",))
        OP("scalar", lambda e: e.activation(lam_t[:], smalls[:, so + S_LAM:so + S_LAM + 64],
           AF.Sigmoid), reads=(smr,), writes=("lam_t",))
        OP("vector", lambda e: e.tensor_tensor(w64a[:], lam_t[:], dt_t[:], A.mult),
           reads=("lam_t", "dt_t"), writes=("w64a",))
        OP("vector", lambda e: e.tensor_copy(gam_bf[:], w64a[:]),
           reads=("w64a",), writes=("gam_bf",))
        OP("scalar", lambda e: e.activation(w64b[:], adisc[:], AF.Exp),
           reads=("adisc",), writes=("w64b",))
        OP("vector", lambda e: e.tensor_scalar(w64c[:], lam_t[:], -1.0, -1.0,
           A.add, A.mult), reads=("lam_t",), writes=("w64c",))
        OP("vector", lambda e: e.tensor_tensor(w64c[:], w64c[:], dt_t[:], A.mult),
           reads=("w64c", "dt_t"), writes=("w64c",))
        OP("vector", lambda e: e.tensor_tensor(w64c[:], w64c[:], w64b[:], A.mult),
           reads=("w64c", "w64b"), writes=("w64c",))
        OP("vector", lambda e: e.tensor_copy(bet_bf[:], w64c[:]),
           reads=("w64c",), writes=("bet_bf",))
        OP("vector", lambda e: e.reduce_sum(m2_t[:], dt_t[:], axis=AX),
           reads=("dt_t",), writes=("m2_t",))
        OP("vector", lambda e: e.tensor_scalar_mul(m2_t[:], m2_t[:], 1.0 / NH),
           reads=("m2_t",), writes=("m2_t",))
        OP("vector", lambda e: e.tensor_scalar_mul(th_t[:], smalls[:, so + S_TH:so + S_TH + 32],
           m2_t[:]), reads=(smr, "m2_t"), writes=("th_t",))
        OP("tensor", lambda e: (npsw(e, 0), e.transpose(psm[0:32, 0:128], th_t[:],
           ident_f[:]))[1], reads=("th_t", "ident_f"), writes=("psm",))
        OP("vector", lambda e: e.tensor_copy(thT[:], psm[0:32, 0:128]),
           reads=("psm",), writes=("thT",))
        OP("vector", lambda e: e.tensor_tensor_scan(angT[:], thT[:], zeros32[:],
           offT[:, 0:1], A.add, A.add), reads=("thT", "zeros32", "offT"), writes=("angT",))
        OP("vector", lambda e: e.tensor_copy(offT[:, 0:1], angT[:, 127:128]),
           reads=("angT",), writes=("offT",))
        OP("tensor", lambda e: (npsw(e, 0), e.transpose(psm[0:128, 128:160], angT[:],
           ident_f[0:32, 0:32]))[1], reads=("angT", "ident_f"), writes=("psm",))
        OP("vector", lambda e: e.tensor_copy(angles[:], psm[0:128, 128:160]),
           reads=("psm",), writes=("angles",))
        def trig(dst_t, quarter):
            OP("vector", lambda e, q=quarter: e.tensor_scalar(tg_u[:], angles[:],
               1.0 / TWO_PI, 256.5 + q, A.mult, A.add), reads=("angles",), writes=("tg_u",))
            OP("vector", lambda e: e.tensor_scalar(tg_k[:], tg_u[:], -0.5, None,
               A.add, A.bypass), reads=("tg_u",), writes=("tg_k",))
            OP("vector", lambda e: e.scalar_tensor_tensor(tg_c[:], tg_u[:], -1.0,
               tg_k[:], A.add, A.is_ge), reads=("tg_u", "tg_k"), writes=("tg_c",))
            OP("vector", lambda e: e.tensor_tensor(tg_c[:], tg_c[:], tg_k[:], A.add),
               reads=("tg_c", "tg_k"), writes=("tg_c",))
            OP("vector", lambda e: e.tensor_tensor(tg_c[:], tg_u[:], tg_c[:], A.subtract),
               reads=("tg_u", "tg_c"), writes=("tg_c",))
            OP("vector", lambda e: e.tensor_scalar(tg_c[:], tg_c[:], -0.5, TWO_PI,
               A.add, A.mult), reads=("tg_c",), writes=("tg_c",))
            OP("vector", lambda e: e.tensor_scalar(tg_c[:], tg_c[:], -PI_LO, PI_LO,
               A.max, A.min), reads=("tg_c",), writes=("tg_c",))
            OP("scalar", lambda e, d=dst_t: e.activation(d[:], tg_c[:], AF.Sin),
               reads=("tg_c",), writes=(dst_t.name,))
        trig(sin_t, 0.0)
        trig(cos_t, 0.25)
        if "trig" in taps:
            OP("vector", lambda e: e.tensor_copy(dump[:, 0:32], sin_t[:]), reads=("sin_t",), writes=("dump",))
            OP("vector", lambda e: e.tensor_copy(dump[:, 32:64], cos_t[:]), reads=("cos_t",), writes=("dump",))
            if t == 0 and p == 0:
                DMA("gpsimd", lambda e: e.dma_start(taps["trig"][:, :], dump[:, 0:64]), reads=("dump",), writes=("tapd",))
        def bc_norm(src_off, bias_t, out_bf):
            OP("scalar", lambda e: e.activation(dump[:, 0:64],
               smalls[:, so + src_off:so + src_off + 64], AF.Square, accum_out=ssq_t[:]),
               reads=(smr,), writes=("dump", "ssq_t"))
            istd_pipe("ssq_t", ssq_t[:], var_t[:], "var_t", 1, 1.0 / NST)
            OP("vector", lambda e, bt=bias_t: e.scalar_tensor_tensor(w64a[:],
               smalls[:, so + src_off:so + src_off + 64], var_t[:], bt[:],
               A.mult, A.add), reads=(smr, "var_t", bias_t.name), writes=("w64a",))
            OP("vector", lambda e: e.tensor_tensor(w64b[:, 0:32], w64a[:, 0:32],
               cos_t[:], A.mult), reads=("w64a", "cos_t"), writes=("w64b",))
            OP("vector", lambda e: e.tensor_tensor(w64c[:, 0:32], w64a[:, 32:64],
               sin_t[:], A.mult), reads=("w64a", "sin_t"), writes=("w64c",))
            OP("vector", lambda e, o=out_bf: e.tensor_tensor(o[:, 0:32], w64b[:, 0:32],
               w64c[:, 0:32], A.subtract), reads=("w64b", "w64c"), writes=(out_bf.name,))
            OP("vector", lambda e: e.tensor_tensor(w64b[:, 0:32], w64a[:, 0:32],
               sin_t[:], A.mult), reads=("w64a", "sin_t"), writes=("w64b",))
            OP("vector", lambda e: e.tensor_tensor(w64c[:, 0:32], w64a[:, 32:64],
               cos_t[:], A.mult), reads=("w64a", "cos_t"), writes=("w64c",))
            OP("vector", lambda e, o=out_bf: e.tensor_tensor(o[:, 32:64], w64b[:, 0:32],
               w64c[:, 0:32], A.add), reads=("w64b", "w64c"), writes=(out_bf.name,))
        bc_norm(S_B, bb_bc, bn_bf)
        bc_norm(S_C, cb_bc, cn_bf)
        if "smalls" in taps and t == 0:
            DMA("gpsimd", lambda e, p=p: e.dma_start(taps["smalls"][p * 128:(p + 1) * 128, :],
                smalls[:, so:so + 288]), reads=(smr,), writes=("tapd",))

    def gen_ssd(t, p, qtoks, ssd_upto=99):
        g0 = t * T + p * 128
        DMA("sync", lambda e: e.dma_start(xh_l0[:], xh_scr[g0 + 1:g0 + 129, :]),
            writes=("xh_l0",), extra=[sch.last_w[f"xh_{t}_{p}"]])
        prev = (f"xh_{t}_{p - 1}" if p > 0 else
                (f"xh_{t - 1}_{NL0 - 1}" if t > 0 else "xh_0_0"))
        DMA("sync", lambda e: e.dma_start(xsh_l0[:, :], xh_scr[g0:g0 + 128, :]),
            writes=("xsh_l0",), extra=[sch.last_w[f"xh_{t}_{p}"], sch.last_w[prev]])
        OP("vector", lambda e: e.tensor_tensor(
           X_l0[:].rearrange("q (h c) -> q h c", h=NH),
           xh_l0[:].rearrange("q (h c) -> q h c", h=NH),
           ap_of(gam_bf, 0, 128, 0, [(1, NH), (0, HD)]), A.mult),
           reads=("xh_l0", "gam_bf"), writes=("X_l0",))
        OP("vector", lambda e: e.tensor_tensor(
           xsh_l0[:].rearrange("q (h c) -> q h c", h=NH),
           xsh_l0[:].rearrange("q (h c) -> q h c", h=NH),
           ap_of(bet_bf, 0, 128, 0, [(1, NH), (0, HD)]), A.mult),
           reads=("xsh_l0", "bet_bf"), writes=("xsh_l0",))
        OP("vector", lambda e: e.tensor_tensor(X_l0[:], X_l0[:], xsh_l0[:], A.add),
           reads=("X_l0", "xsh_l0"), writes=("X_l0",))
        DMA("sync", lambda e: e.dma_start(xsh_l0[:], sz_scr[g0:g0 + 128, :]),
            writes=("xsh_l0",), extra=[sch.last_w[f"sz_{t}_{p}"]])
        if "X" in taps and t == 0 and p == 0:
            DMA("gpsimd", lambda e: e.dma_start(taps["X"][:, :], X_l0[:]), reads=("X_l0",), writes=("tapd",))
        OP("tensor", lambda e: (npsw(e, 0), e.matmul(psm[:, 0:64], u2[:], adisc[:],
           start=True, stop=True))[1], reads=("u2", "adisc"), writes=("psm",))
        OP("vector", lambda e: e.tensor_copy(acs[:], psm[:, 0:64]),
           reads=("psm",), writes=("acs",))
        OP("scalar", lambda e: e.activation(ea_t[:], acs[:], AF.Exp),
           reads=("acs",), writes=("ea_t",))
        OP("tensor", lambda e: (npsw(e, 0), e.transpose(psm[0:64, 128:256], acs[:],
           ident_f[:]))[1], reads=("acs", "ident_f"), writes=("psm",))
        OP("vector", lambda e: e.tensor_copy(acsT[:], psm[0:64, 128:256]),
           reads=("psm",), writes=("acsT",))
        if "acs" in taps and t == 0 and p == 0:
            DMA("gpsimd", lambda e: e.dma_start(taps["acs"][:, :], acs[:]), reads=("acs",), writes=("tapd",))
        OP("tensor", lambda e: (npsw(e, 0), e.transpose(ptr_bf[0:64, 64:192],
           bn_bf[:, :], ident_bf[:]))[1], reads=("bn_bf", "ident_bf"), writes=("psm",))
        OP("vector", lambda e: e.tensor_copy(bnT[:, :], ptr_bf[0:64, 64:192]),
           reads=("psm",), writes=("bnT",))
        OP("tensor", lambda e: (npsw(e, 0), e.transpose(ptr_bf[0:64, 64:192],
           cn_bf[:, :], ident_bf[:]))[1], reads=("cn_bf", "ident_bf"), writes=("psm",))
        OP("vector", lambda e: e.tensor_copy(cnT_e[0:64, :], ptr_bf[0:64, 64:192]),
           reads=("psm",), writes=("cnT_e",))
        OP("tensor", lambda e: (npsw(e, 0), e.transpose(ptr_bf[64:128, 64:192],
           cn_bf[:, :], ident_bf[:]))[1], reads=("cn_bf", "ident_bf"), writes=("psm",))
        OP("vector", lambda e: e.tensor_copy(cnT_o[64:128, :], ptr_bf[64:128, 64:192]),
           reads=("psm",), writes=("cnT_o",))
        for c in range(2):
            cs = c * 64
            OP("vector", lambda e, cs=cs: e.tensor_scalar(dectmp[:, cs:cs + 64],
               acsT[:, cs:cs + 64], acsT[:, cs + 63:cs + 64], None, A.subtract, A.bypass),
               reads=("acsT",), writes=("dectmp",))
            OP("scalar", lambda e, cs=cs: e.activation(dect[:, cs:cs + 64],
               dectmp[:, cs:cs + 64], AF.Exp, scale=-1.0), reads=("dectmp",), writes=("dect",))
            OP("tensor", lambda e, cs=cs: (npsw(e, 0), e.transpose(ptr_bf[cs:cs + 64, 0:64],
               dect[:, cs:cs + 64], ident_bf[0:64, 0:64]))[1],
               reads=("dect", "ident_bf"), writes=("psm",))
            OP("vector", lambda e, cs=cs: e.tensor_copy(dec_tm[cs:cs + 64, :],
               ptr_bf[cs:cs + 64, 0:64]), reads=("psm",), writes=("dec_tm",))
            OP("tensor", lambda e, cs=cs: (npsw(e, 0), e.matmul(psm[cs:cs + 64, 256:320],
               bnT[:, cs:cs + 64], cnT_e[0:64, cs:cs + 64], start=True, stop=True))[1],
               reads=("bnT", "cnT_e"), writes=("psm",))
            OP("vector", lambda e, cs=cs: e.tensor_tensor(gu_bd[cs:cs + 64, cs:cs + 64],
               psm[cs:cs + 64, 256:320], u64[:], A.mult),
               reads=("psm", "u64"), writes=("gu_bd",))
            OP("vector", lambda e, cs=cs, c=c: e.tensor_tensor(
               ap_of(bd2, cs, 64, c * 4096, [(64, NH), (1, NST)]),
               ap_of(bn_bf, cs, 64, 0, [(0, NH), (1, NST)]),
               ap_of(dec_tm, cs, 64, 0, [(1, NH), (0, NST)]), A.mult),
               reads=("bn_bf", "dec_tm"), writes=("bd2",))

        if ssd_upto < 3:
            return
        DMA("gpsimd", lambda e: e.dma_start(bounce[2:3, 0:NH], ea_t[63:64, :]),
            reads=("ea_t",), writes=("dkrow0",))
        DMA("gpsimd", lambda e: e.dma_start(bounce[3:4, 0:NH], ea_t[127:128, :]),
            reads=("ea_t",), writes=("dkrow1",))
        for c in range(2):
            for par in range(2):
                DMA("sync", lambda e, c=c, par=par: e.dma_start(
                    decay_arr[par * 64:(par + 1) * 64, c * 32:(c + 1) * 32],
                    bass.AP(bounce, (2 + c) * 4096 + par, [[0, 64], [2, 32]]),
                    allow_slow_non_contiguous=True),
                    reads=(f"dkrow{c}",), writes=("decay_arr",))
        DMA("gpsimd", lambda e: e.dma_start(
            bass.AP(bounce, 8 * 4096, [[64, 64], [1, 64]]),
            acsT[:, 0:64]), reads=("acsT",), writes=("acsf0",))
        DMA("gpsimd", lambda e: e.dma_start(
            bass.AP(bounce, 9 * 4096, [[64, 64], [1, 64]]),
            acsT[:, 64:128]), reads=("acsT",), writes=("acsf1",))
        if ssd_upto < 4:
            return
        for q in range(4):
            h0 = q * 16
            for c in range(2):
                DMA("sync", lambda e, c=c, h0=h0: e.dma_start(dsrc[c * 64:(c + 1) * 64, 0:1024],
                    bass.AP(bounce, (8 + c) * 4096 + h0 * 64, [[0, 64], [1, 1024]])),
                    reads=(f"acsf{c}",), writes=("dsrc",))
                OP("vector", lambda e, c=c, h0=h0: e.tensor_tensor(
                   dsrc[c * 64:(c + 1) * 64, 0:1024].rearrange("q2 (h l) -> q2 h l", h=16),
                   dsrc[c * 64:(c + 1) * 64, 0:1024].rearrange("q2 (h l) -> q2 h l", h=16),
                   ap_of(acs, c * 64, 64, h0, [(1, 16), (0, 64)]), A.subtract),
                   reads=("dsrc", "acs"), writes=("dsrc",))
                OP("scalar", lambda e, c=c: e.activation(
                   ap_of(lpair, c * 64, 64, c * 64, [(128, 16), (1, 64)]),
                   dsrc[c * 64:(c + 1) * 64, 0:1024], AF.Exp), reads=("dsrc",), writes=("lpair",))
            OP("vector", lambda e: e.tensor_tensor(
               lpair[:].rearrange("q2 (h l) -> q2 h l", h=16),
               lpair[:].rearrange("q2 (h l) -> q2 h l", h=16),
               ap_of(gu_bd, 0, 128, 0, [(0, 16), (1, 128)]), A.mult),
               reads=("lpair", "gu_bd"), writes=("lpair",))
            if ssd_upto < 5:
                continue
            for oo in range(2):
                gen_oct(t, p, q * 2 + oo)
        if "sst" in taps and t == 0 and p == 1:
            DMA("gpsimd", lambda e: e.dma_start(taps["sst"][:, :], s_st[:]), reads=("s_st",), writes=("tapd",))

    def gen_oct(t, p, oct_i, ou=99):
        g0 = t * T + p * 128
        hbase = oct_i * 8
        oc = hbase * HD
        socol = oct_i * 256
        par2 = oct_i % 2
        pY, pO, pS = (P[0], P[1], P[2]) if par2 == 0 else (P[3], P[4], P[5])
        s_bf = s_bf2[par2]
        eab = eab2[par2]
        xd512 = xd2[par2]
        sres = f"s_st_{oct_i}"
        OP("vector", lambda e: e.tensor_copy(s_bf[:, :], s_st[:, socol:socol + 256]),
           reads=(sres,), writes=(s_bf.name,))
        OP("vector", lambda e: e.tensor_tensor(
           xd512[:].rearrange("q (h c) -> q h c", h=8),
           xh_l0[:, oc:oc + 512].rearrange("q (h c) -> q h c", h=8),
           ap_of(dpar_bc, 0, 128, hbase, [(1, 8), (0, HD)]), A.mult),
           reads=("xh_l0", "dpar_bc"), writes=(xd512.name,))
        OP("tensor", lambda e: (npsw(e, 0), e.matmul(pY[:, 0:512], ident_bf[:],
           xd512[:, 0:512], start=True, stop=False))[1],
           reads=("ident_bf", xd512.name), writes=(pY.name,))
        def ydiag(e):
            inst = None
            npsw(e, 0)
            for j in range(8):
                h = hbase + j
                inst = e.matmul(pY[:, j * 64:(j + 1) * 64],
                                lpair[:, (h % 16) * 128:(h % 16 + 1) * 128],
                                X_l0[:, h * HD:(h + 1) * HD],
                                start=False, stop=False)
            return inst
        OP("tensor", ydiag, reads=("lpair", "X_l0"), writes=(pY.name,))
        def close_py(e):
            npsw(e, 0)
            return e.matmul(pY[:, 0:512], ident_bf[:], xd512[:, 0:512], start=False, stop=True)
        if ou < 2:
            OP("tensor", close_py, reads=("ident_bf", xd512.name), writes=(pY.name,))
            return
        def yoff(e, c, start):
            inst = None
            npsw(e, 0)
            for j in range(8):
                h = hbase + j
                ct2 = cnT_e if h % 2 == 0 else cnT_o
                h2 = j // 2
                inst = e.matmul(pO[c * 64:c * 64 + 64, j * 64:(j + 1) * 64],
                                ct2[:, c * 64:c * 64 + 64],
                                s_bf[:, h2 * 64:(h2 + 1) * 64],
                                start=True, stop=True)
            return inst
        def states(e, c, start):
            inst = None
            npsw(e, 0)
            for j in range(8):
                h = hbase + j
                par = (h % 2) * 64
                h2 = j // 2
                inst = e.matmul(pS[par:par + 64, c * 256 + h2 * 64:c * 256 + (h2 + 1) * 64],
                                bd2[:, c * 4096 + h * HD:c * 4096 + (h + 1) * HD],
                                X_l0[:, h * HD:(h + 1) * HD],
                                start=True, stop=True)
            return inst
        OP("tensor", lambda e: yoff(e, 0, True), reads=("cnT_e", "cnT_o", s_bf.name), writes=(pO.name,))
        if ou < 3:
            OP("tensor", close_py, reads=("ident_bf", xd512.name), writes=(pY.name,))
            return
        OP("tensor", lambda e: states(e, 0, True), reads=("bd2", "X_l0"), writes=(pS.name,))
        OP("vector", lambda e: e.tensor_tensor(
           s_st[:, socol:socol + 256].rearrange("q (h2 c) -> q h2 c", h2=4),
           s_st[:, socol:socol + 256].rearrange("q (h2 c) -> q h2 c", h2=4),
           ap_of(decay_arr, 0, 128, 0 * 32 + oct_i * 4, [(1, 4), (0, 64)]), A.mult),
           reads=(sres, "decay_arr"), writes=(sres,))
        OP("vector", lambda e: e.tensor_tensor(s_st[:, socol:socol + 256],
           s_st[:, socol:socol + 256], pS[:, 0:256], A.add),
           reads=(sres, pS.name), writes=(sres,))
        OP("vector", lambda e: e.tensor_copy(s_bf[:, :], s_st[:, socol:socol + 256]),
           reads=(sres,), writes=(s_bf.name,))
        if ou < 4:
            OP("tensor", close_py, reads=("ident_bf", xd512.name), writes=(pY.name,))
            return
        OP("tensor", lambda e: yoff(e, 1, False), reads=("cnT_e", "cnT_o", s_bf.name), writes=(pO.name,))
        OP("tensor", lambda e: states(e, 1, False), reads=("bd2", "X_l0"), writes=(pS.name,))
        OP("vector", lambda e: e.tensor_tensor(
           s_st[:, socol:socol + 256].rearrange("q (h2 c) -> q h2 c", h2=4),
           s_st[:, socol:socol + 256].rearrange("q (h2 c) -> q h2 c", h2=4),
           ap_of(decay_arr, 0, 128, 1 * 32 + oct_i * 4, [(1, 4), (0, 64)]), A.mult),
           reads=(sres, "decay_arr"), writes=(sres,))
        OP("vector", lambda e: e.tensor_tensor(s_st[:, socol:socol + 256],
           s_st[:, socol:socol + 256], pS[:, 256:512], A.add),
           reads=(sres, pS.name), writes=(sres,))
        if ou < 5:
            OP("tensor", close_py, reads=("ident_bf", xd512.name), writes=(pY.name,))
            return
        OP("vector", lambda e: e.tensor_tensor(
           eab[:].rearrange("q (h c) -> q h c", h=8),
           pO[:].rearrange("q (h c) -> q h c", h=8),
           ap_of(ea_t, 0, 128, hbase, [(1, 8), (0, 64)]), A.mult),
           reads=(pO.name, "ea_t"), writes=(eab.name,))
        OP("tensor", lambda e: (npsw(e, 0), e.matmul(pY[:, 0:512], ident_bf[:], eab[:],
           start=False, stop=True))[1], reads=("ident_bf", eab.name), writes=(pY.name,))
        if ou < 6:
            return
        st = stage[oct_i % 2]
        OP("vector", lambda e, st=st: e.tensor_tensor(st[:, :], pY[:, 0:512],
           xsh_l0[:, oc:oc + 512], A.mult), reads=(pY.name, "xsh_l0"), writes=(st.name,))
        OP("scalar", lambda e, st=st: e.activation(dump[:, 0:512], st[:, :], AF.Square,
           accum_out=ssq8[:, p * 8 + oct_i:p * 8 + oct_i + 1]),
           reads=(st.name,), writes=("dump", f"ssq8_{p}"))
        DMA("gpsimd", lambda e, st=st: e.dma_start(yg_scr[g0:g0 + 128, oc:oc + 512],
            st[:, :]), reads=(st.name,), writes=(f"yg_{t}_{p}_{oct_i}",))

    def gen_istd(t, p):
        OP("vector", lambda e, p=p: e.reduce_sum(ssq_t[:], ssq8[:, p * 8:(p + 1) * 8],
           axis=AX), reads=(f"ssq8_{p}",), writes=("ssq_t",))
        istd_pipe("ssq_t", ssq_t[:], var_t[:], "var_t", 1, 1.0 / DIN)
        OP("vector", lambda e, p=p: e.tensor_copy(istd_t[:, p:p + 1], var_t[:]),
           reads=("var_t",), writes=("istd_t",))

    def gen_outproj_ffn(t, qtoks):
        trow = t * T
        yg_deps = [sch.last_w[f"yg_{t}_{p}_{o}"] for p in range(NL0) for o in range(8)]
        for lp in range(2):
            for k in range(32):
                import os
                _db = os.environ.get('ODB', '1') == '1'
                wg, lb = (wgt[k % 2], lhsb[k % 2]) if _db else (wgt[0], lhsb[0])
                DMA("sync", lambda e, k=k, wg=wg: e.dma_start_transpose(
                    wg[:, 0:D], wq_out[0:D, k * 128:(k + 1) * 128]),
                    writes=(wg.name,), extra=rows_dep(qtoks["wq_out"], 0, D))
                DMA("sync", lambda e, k=k, lb=lb, trow=trow: e.dma_start_transpose(
                    lb[:, 0:T], yg_scr[trow:trow + T, k * 128:(k + 1) * 128]),
                    writes=(lb.name,), extra=yg_deps)
                def mmo(e, k=k, lp=lp, wg=wg, lb=lb):
                    inst = None
                    npsw(e, 0)
                    for i in range(2):
                        for dc in range(4):
                            pp = P[i * 4 + dc] if i * 4 + dc < 7 else psm
                            inst = e.matmul(pp[:, 0:512],
                                            lb[:, (lp * 2 + i) * 128:(lp * 2 + i + 1) * 128],
                                            wg[:, dc * 512:(dc + 1) * 512],
                                            start=(k == 0), stop=(k == 31))
                    return inst
                OP("tensor", mmo, reads=(wg.name, lb.name),
                   writes=tuple(P[i].name for i in range(7)) + ("psm",))
            for i in range(2):
                p_ = lp * 2 + i
                g0 = trow + p_ * 128
                gen_istd(t, p_)
                buf = xl0 if i == 0 else scr2
                for dc in range(4):
                    pp = P[i * 4 + dc] if i * 4 + dc < 7 else psm
                    OP("vector", lambda e, dc=dc, pp=pp, b=buf, p_=p_:
                       e.tensor_scalar_mul(b[:, dc * 512:(dc + 1) * 512], pp[:, 0:512],
                                           istd_t[:, p_:p_ + 1]),
                       reads=(pp.name, "istd_t"), writes=(buf.name,))
                if "attnb" in taps and t == 0 and p_ == 0:
                    DMA("gpsimd", lambda e, b=buf: e.dma_start(taps["attnb"][:, :], b[:, 0:D]),
                        reads=(buf.name,), writes=("tapd",))
                if "istd4" in taps and t == 0 and p_ == 0:
                    OP("vector", lambda e: e.tensor_copy(dump[:, 0:8], ssq8[:, 0:8]),
                       reads=("ssq8",), writes=("dump",))
                    DMA("gpsimd", lambda e: e.dma_start(taps["istd4"][:, :], dump[:, 0:8]),
                        reads=("dump",), writes=("tapd",))
                ln_stats(buf[:, 0:D], buf.name)
                DMA("sync", lambda e, g0=g0: e.dma_start(scr3[:], x_in[g0:g0 + 128, :]),
                    writes=("qnat",))
                OP("vector", lambda e, b=buf: e.tensor_scalar(b[:, 0:D], b[:, 0:D],
                   mean_t[:], var_t[:], A.subtract, A.mult),
                   reads=(buf.name, "mean_t", "var_t"), writes=(buf.name,))
                OP("vector", lambda e, b=buf: e.tensor_tensor(b[:, 0:D], b[:, 0:D],
                   scr3[:, 0:D], A.add), reads=(buf.name, "qnat"), writes=(buf.name,))
                DMA("gpsimd", lambda e, g0=g0, b=buf: e.dma_start(
                    x2_scr[g0:g0 + 128, :], b[:, 0:D]), reads=(buf.name,),
                    writes=(f"x2_{t}_{p_}",))
                ln_stats(buf[:, 0:D], buf.name)
                OP("vector", lambda e, b=buf: e.tensor_scalar(dump[:, 0:D], b[:, 0:D],
                   mean_t[:], var_t[:], A.subtract, A.mult),
                   reads=(buf.name, "mean_t", "var_t"), writes=("dump",))
                OP("vector", lambda e: e.reduce_max(m2_t[:], dump[:, 0:D], axis=AX,
                   apply_absolute_value=True), reads=("dump",), writes=("m2_t",))
                OP("vector", lambda e: e.tensor_scalar(m2_t[:], m2_t[:], 1.0 / 127.0,
                   1e-12, A.mult, A.max), reads=("m2_t",), writes=("m2_t",))
                OP("vector", lambda e, p_=p_: e.tensor_copy(xs_t[:, p_:p_ + 1], m2_t[:]),
                   reads=("m2_t",), writes=("xs_t",))
                OP("vector", lambda e: e.reciprocal(var_t[:], m2_t[:]),
                   reads=("m2_t",), writes=("var_t",))
                OP("vector", lambda e: e.tensor_scalar(qu[:, 0:D], dump[:, 0:D],
                   var_t[:], 128.5, A.mult, A.add), reads=("dump", "var_t"), writes=("qu",))
                OP("vector", lambda e: e.tensor_scalar(qk[:, 0:D], qu[:, 0:D], -0.5,
                   None, A.add, A.bypass), reads=("qu",), writes=("qk",))
                OP("vector", lambda e: e.scalar_tensor_tensor(qu[:, 0:D], qu[:, 0:D],
                   -1.0, qk[:, 0:D], A.add, A.is_ge), reads=("qu", "qk"), writes=("qu",))
                OP("vector", lambda e: e.tensor_tensor(qu[:, 0:D], qu[:, 0:D],
                   qk[:, 0:D], A.add), reads=("qu", "qk"), writes=("qu",))
                OP("vector", lambda e: e.tensor_scalar(qfq[:, 0:D], qu[:, 0:D],
                   -128.0, None, A.add, A.bypass), reads=("qu",), writes=("qfq",))
                DMA("gpsimd", lambda e, g0=g0: e.dma_start(xq_scr[g0:g0 + 128, :],
                    qfq[:, 0:D]), reads=("qfq",), writes=(f"xq_{t}_{p_}",))
        for k in range(16):
            DMA("sync", lambda e, k=k, trow=trow: e.dma_start_transpose(
                xnT[:, k * T:(k + 1) * T], xq_scr[trow:trow + T, k * 128:(k + 1) * 128]),
                writes=("xnT",),
                extra=[sch.last_w[f"xq_{t}_{p}"] for p in range(NL0)])
        for lp in range(2):
            for k in range(16):
                wg = wgt[k % 2]
                DMA("sync", lambda e, k=k, wg=wg: e.dma_start_transpose(
                    wg[:, 0:D], wq_ffn[0:D, k * 128:(k + 1) * 128]),
                    writes=(wg.name,), extra=rows_dep(qtoks["wq_ffn"], 0, D))
                def mmf(e, k=k, lp=lp, wg=wg):
                    inst = None
                    npsw(e, 0)
                    for i in range(2):
                        for oc in range(4):
                            pp = P[i * 4 + oc] if i * 4 + oc < 7 else psm
                            inst = e.matmul(pp[:, 0:512],
                                            xnT[:, k * T + (lp * 2 + i) * 128:k * T + (lp * 2 + i + 1) * 128],
                                            wg[:, oc * 512:(oc + 1) * 512],
                                            start=(k == 0), stop=(k == 15))
                    return inst
                OP("tensor", mmf, reads=(wg.name, "xnT"),
                   writes=tuple(P[i].name for i in range(7)) + ("psm",))
            for i in range(2):
                p_ = lp * 2 + i
                g0 = trow + p_ * 128
                buf = xl0 if i == 0 else scr2
                for oc in range(4):
                    pp = P[i * 4 + oc] if i * 4 + oc < 7 else psm
                    OP("vector", lambda e, oc=oc, pp=pp, b=buf, p_=p_:
                       e.scalar_tensor_tensor(b[:, oc * 512:(oc + 1) * 512], pp[:, 0:512],
                                              xs_t[:, p_:p_ + 1],
                                              ws_bc[:, oc * 512:(oc + 1) * 512],
                                              A.mult, A.mult),
                       reads=(pp.name, "xs_t", "ws_bc"), writes=(buf.name,))
                OP("vector", lambda e, b=buf: e.tensor_tensor(b[:, 0:D], b[:, 0:D],
                   bias_bc[:, 0:D], A.add), reads=(buf.name, "bias_bc"), writes=(buf.name,))
                ln_stats(buf[:, 0:D], buf.name)
                DMA("sync", lambda e, g0=g0: e.dma_start(scr3[:], x2_scr[g0:g0 + 128, :]),
                    writes=("qnat",), extra=[sch.last_w[f"x2_{t}_{p_}"]])
                OP("vector", lambda e, b=buf: e.tensor_scalar(b[:, 0:D], b[:, 0:D],
                   mean_t[:], var_t[:], A.subtract, A.mult),
                   reads=(buf.name, "mean_t", "var_t"), writes=(buf.name,))
                OP("vector", lambda e, b=buf: e.tensor_tensor(b[:, 0:D], b[:, 0:D],
                   scr3[:, 0:D], A.add), reads=(buf.name, "qnat"), writes=(buf.name,))
                DMA("gpsimd", lambda e, g0=g0, b=buf: e.dma_start(
                    y_out[g0:g0 + 128, :], b[:, 0:D]), reads=(buf.name,),
                    writes=(f"y_{t}_{p_}",))

    qtoks = {"wq_in": [], "wq_out": [], "wq_ffn": []}
    gen_quant_win(win_jobs(0, 4384), qtoks)
    gen_ln_x(0)
    gen_xnT(0)
    gen_inproj(0, qtoks, fcs=set(range(0, 9)))
    gen_quant_win(win_jobs(4384, NPROJ), qtoks)
    gen_quant_wout(qtoks)
    gen_inproj(0, qtoks, fcs=set(range(9, 17)))
    for p in range(NL0):
        gen_smalls(0, p)
        gen_ssd(0, p, qtoks, ssd_upto)
    gen_quant_ffn(qtoks)
    gen_outproj_ffn(0, qtoks)
    for t in range(1, NT):
        gen_ln_x(t)
        gen_xnT(t)
        gen_inproj(t, qtoks)
        for p in range(NL0):
            gen_smalls(t, p)
            gen_ssd(t, p, qtoks, ssd_upto)
        gen_outproj_ffn(t, qtoks)
    OP("sync", lambda e: e.nop(),
       extra=[sch.last_w[f"y_{t}_{p}"] for t in range(NT) for p in range(NL0)])
    with nc.Block() as block:
        sch.emit(block)
    print("SBUF remaining:", nc.sbuf_bytes_remaining, "PSUM banks remaining:", nc.psum_banks_remaining)
    ninstr = {e: len(sch.ops[e]) for e in sch.ENGINES}
    print("ops per engine:", ninstr, "total", sum(ninstr.values()))
    es.close()
    return nc


_NC_CACHE = [None]

def kernel(L0_attn_x, L0_attn_W_in, L0_attn_W_theta, L0_attn_W_lambda, L0_attn_W_out,
           L0_attn_B_bias, L0_attn_C_bias, L0_attn_A_log, L0_attn_D_param,
           L0_attn_dt_bias, L0_ffn_weight, L0_ffn_bias):
    from concourse.bass_utils import run_bass_kernel_spmd
    if _NC_CACHE[0] is None:
        _NC_CACHE[0] = build()
    nc = _NC_CACHE[0]
    x = np.ascontiguousarray(np.asarray(L0_attn_x, np.float32))
    shared = {
        "w_in": np.ascontiguousarray(np.asarray(L0_attn_W_in, np.float32)),
        "w_theta": np.ascontiguousarray(np.asarray(L0_attn_W_theta, np.float32)),
        "w_lambda": np.ascontiguousarray(np.asarray(L0_attn_W_lambda, np.float32)),
        "w_out": np.ascontiguousarray(np.asarray(L0_attn_W_out, np.float32)),
        "b_bias": np.asarray(L0_attn_B_bias, np.float32).reshape(1, NST),
        "c_bias": np.asarray(L0_attn_C_bias, np.float32).reshape(1, NST),
        "a_log": np.asarray(L0_attn_A_log, np.float32).reshape(1, NH),
        "d_param": np.asarray(L0_attn_D_param, np.float32).reshape(1, NH),
        "dt_bias": np.asarray(L0_attn_dt_bias, np.float32).reshape(1, NH),
        "ffn_w": np.ascontiguousarray(np.asarray(L0_ffn_weight, np.float32)),
        "ffn_b": np.asarray(L0_ffn_bias, np.float32).reshape(1, D),
    }
    in_maps = [dict(shared, x=x[i]) for i in range(8)]
    res = run_bass_kernel_spmd(nc, in_maps, core_ids=list(range(8)))
    return np.stack([res.results[i]["y"] for i in range(8)], axis=0)


# version 07
# speedup vs baseline: 1.1446x; 1.1446x over previous
import math
from contextlib import ExitStack

import numpy as np
import concourse.bass as bass
import concourse.mybir as mybir

F32 = mybir.dt.float32
BF16 = mybir.dt.bfloat16
I32 = mybir.dt.int32
A = mybir.AluOpType
AF = mybir.ActivationFunctionType
AX = mybir.AxisListType.X

L, D, DIN, NH, HD, NST = 2048, 2048, 4096, 64, 64, 64
T = 512
NT = L // T
NL0 = T // 128
RMS_EPS = 1.1920929e-07
LN_EPS = 1e-5
TWO_PI = 2.0 * math.pi
PI_LO = 3.1415925
NPROJ = 8480

R_SM, R_XC, R_Z = 0, 288, 288 + 4096
FC = [(R_SM, 288, "sm")] + [(R_XC + i * 512, 512, "xc") for i in range(8)] + \
     [(R_Z + i * 512, 512, "z") for i in range(8)]
S_B, S_C, S_DT, S_TH, S_LAM = 0, 64, 128, 192, 224


def ap_of(t, base_part, npart, coloff, dims):
    row = int(np.prod(t.shape[1:]))
    return bass.AP(t, base_part * row + coloff,
                   [[row, npart]] + [[s, c] for (s, c) in dims])


class Sched:
    ENGINES = ("sync", "gpsimd", "vector", "scalar", "tensor")
    DQ = ("sync", "gpsimd", "scalar")

    def __init__(self, nc, es):
        self.nc = nc
        self.es = es
        self.ops = {e: [] for e in self.ENGINES}
        self.sem = {e: es.enter_context(nc.semaphore(f"s_{e}")) for e in self.ENGINES}
        self.cnt = {e: 0 for e in self.ENGINES}
        self.chsem = {}
        self.chcnt = {}
        self.last_w = {}
        self.readers = {}

    def _channel(self, res):
        parts = res.split("_")
        if len(parts) > 1 and all(p.isdigit() for p in parts[1:]):
            return parts[0]
        return res

    def _get_ch(self, ch):
        if ch not in self.chsem:
            self.chsem[ch] = self.es.enter_context(self.nc.semaphore(f"ch_{ch}"))
            self.chcnt[ch] = 0
        return self.chsem[ch]

    def _dep(self, reads, writes):
        toks = []
        for r in reads:
            if r in self.last_w:
                toks.append(self.last_w[r])
        for r in writes:
            if r in self.last_w:
                toks.append(self.last_w[r])
            toks.extend(self.readers.get(r, ()))
        out = []
        seen_ch = set()
        for t in toks:
            if t[0] == "D":
                ch = t[1]
                if ch not in seen_ch:
                    seen_ch.add(ch)
                    out.append(("D", ch, self.chcnt[ch]))
            else:
                out.append(t)
        return out

    def _post(self, tok, reads, writes):
        for r in writes:
            self.last_w[r] = tok
            self.readers[r] = []
        for r in reads:
            self.readers.setdefault(r, []).append(tok)

    def op(self, eng, fn, reads=(), writes=(), extra=()):
        deps = self._dep(tuple(reads) + tuple(x for x in ()), writes)
        for t in extra:
            if t[0] == "D":
                deps.append(("D", t[1], self.chcnt[t[1]]))
            else:
                deps.append(t)
        self.cnt[eng] += 1
        tok = ("c", eng, self.cnt[eng])
        self.ops[eng].append((None, fn, deps, tuple(reads), tuple(writes)))
        self._post(tok, reads, writes)
        return tok

    def dma(self, eng, fn, reads=(), writes=(), extra=()):
        assert eng in self.DQ and writes
        ch = self._channel(writes[0])
        self._get_ch(ch)
        deps = self._dep(reads, writes)
        for t in extra:
            if t[0] == "D":
                deps.append(("D", t[1], self.chcnt[t[1]]))
            else:
                deps.append(t)
        self.chcnt[ch] += 16
        tok = ("D", ch, self.chcnt[ch])
        self.ops[eng].append((ch, fn, deps, tuple(reads), tuple(writes)))
        self._post(tok, reads, writes)
        return tok

    def emit(self, block):
        sched = self

        def mk(e_name):
            def body(eng):
                waited = {}
                dirty = set()
                dirty_r = set()
                for ch, fn, deps, rds, wrs in sched.ops[e_name]:
                    for dep in deps:
                        k, src, val = dep
                        if k == "c" and src == e_name:
                            continue
                        key = (k, src)
                        if waited.get(key, -1) >= val:
                            continue
                        waited[key] = val
                        eng.wait_ge(sched.sem[src] if k == "c" else sched.chsem[src], val)
                    if e_name in ("vector", "scalar", "gpsimd"):
                        if (any(r in dirty for r in rds) or any(w in dirty for w in wrs)
                                or any(w in dirty_r for w in wrs)):
                            eng.drain()
                            dirty.clear()
                            dirty_r.clear()
                    inst = fn(eng)
                    if ch is not None:
                        if e_name in ("vector", "scalar", "gpsimd"):
                            dirty.update(wrs)
                            dirty_r.update(rds)
                        inst.then_inc(sched.chsem[ch], 16)
                    else:
                        if e_name in ("vector", "scalar", "gpsimd"):
                            dirty.update(wrs)
                            dirty_r.update(rds)
                        inst.then_inc(sched.sem[e_name], 1)
            return body

        block.sync(mk("sync"))
        block.gpsimd(mk("gpsimd"))
        block.vector(mk("vector"))
        block.scalar(mk("scalar"))
        block.tensor(mk("tensor"))


def build(dbg=(), ssd_upto=99, dump_scr=False):
    nc = bass.Bass("TRN2", target_bir_lowering=False)
    es = ExitStack()
    dbg = set(dbg)

    x_in = nc.dram_tensor("x", [L, D], F32, kind="ExternalInput")
    w_in = nc.dram_tensor("w_in", [8384, D], F32, kind="ExternalInput")
    w_th = nc.dram_tensor("w_theta", [32, D], F32, kind="ExternalInput")
    w_lm = nc.dram_tensor("w_lambda", [64, D], F32, kind="ExternalInput")
    w_out = nc.dram_tensor("w_out", [D, DIN], F32, kind="ExternalInput")
    b_bias = nc.dram_tensor("b_bias", [1, NST], F32, kind="ExternalInput")
    c_bias = nc.dram_tensor("c_bias", [1, NST], F32, kind="ExternalInput")
    a_log = nc.dram_tensor("a_log", [1, NH], F32, kind="ExternalInput")
    d_par = nc.dram_tensor("d_param", [1, NH], F32, kind="ExternalInput")
    dt_bias_d = nc.dram_tensor("dt_bias", [1, NH], F32, kind="ExternalInput")
    fw = nc.dram_tensor("ffn_w", [D, D], F32, kind="ExternalInput")
    fb = nc.dram_tensor("ffn_b", [1, D], F32, kind="ExternalInput")
    y_out = nc.dram_tensor("y", [L, D], F32, kind="ExternalOutput")

    KS = "ExternalOutput" if dump_scr else "Internal"
    wq_in = nc.dram_tensor("wq_in", [D, NPROJ], BF16, kind=KS)
    wq_out = nc.dram_tensor("wq_out", [DIN, D], BF16, kind=KS)
    wq_ffn = nc.dram_tensor("wq_ffn", [D, D], BF16, kind=KS)
    xn_scr = nc.dram_tensor("xn_scr", [L, D], BF16, kind=KS)
    xh_scr = nc.dram_tensor("xh_scr", [L + 1, DIN], BF16, kind=KS)
    sz_scr = nc.dram_tensor("sz_scr", [L, DIN], BF16, kind=KS)
    yg_scr = nc.dram_tensor("yg_scr", [L, DIN], BF16, kind=KS)
    xq_scr = nc.dram_tensor("xq_scr", [L, D], BF16, kind=KS)
    x2_scr = nc.dram_tensor("x2_scr", [L, D], F32, kind=KS)
    bounce = nc.dram_tensor("bounce", [16, 4096], F32, kind="Internal")

    taps = {}
    def tap(name, shape, dt):
        if name in dbg:
            taps[name] = nc.dram_tensor("tap_" + name, shape, dt, kind="ExternalOutput")
        return taps.get(name)

    tap("smalls", [T, 288], F32)
    tap("acs", [128, NH], F32)
    tap("X", [128, DIN], BF16)
    tap("bd", [128, NH * NST], BF16)
    tap("M", [128, 32 * 128], BF16)
    tap("sst", [128, 2048], F32)
    tap("trig", [128, 64], F32)
    tap("attnb", [128, D], F32)
    tap("istd4", [128, 8], F32)

    sbT = lambda name, shape, dt: es.enter_context(nc.sbuf_tensor(name, shape, dt))
    psT = lambda name, shape, dt: es.enter_context(nc.psum_tensor(name, shape, dt))

    ident_bf = sbT("ident_bf", [128, 128], BF16)
    ident_f = sbT("ident_f", [128, 128], F32)
    u2 = sbT("u2", [128, 128], F32)
    u64 = sbT("u64", [64, 64], F32)
    dtb_bc = sbT("dtb_bc", [128, NH], F32)
    bb_bc = sbT("bb_bc", [128, NST], F32)
    cb_bc = sbT("cb_bc", [128, NST], F32)
    negae_bc = sbT("negae_bc", [128, NH], F32)
    dpar_bc = sbT("dpar_bc", [128, NH], BF16)
    ws_bc = sbT("ws_bc", [128, D], F32)
    bias_bc = sbT("bias_bc", [128, D], F32)
    arow = sbT("arow", [1, NH], F32)
    arow2 = sbT("arow2", [1, NH], F32)
    cst = sbT("cst", [128, 4], F32)
    qnat = sbT("qnat", [128, D], F32)
    qu = sbT("qu", [128, D], F32)
    qk = sbT("qk", [128, D], I32)
    qfq = sbT("qfq", [128, D], BF16)
    qgm = sbT("qgm", [128, 32], F32)
    qs = sbT("qs", [128, 32], F32)
    qrs = sbT("qrs", [128, 32], F32)
    gnat = sbT("gnat", [128, 1024], F32)
    gu2 = sbT("gu2", [128, 1024], F32)
    gk = sbT("gk", [128, 1024], I32)
    gfq = sbT("gfq", [128, 1024], BF16)
    ggm = sbT("ggm", [128, 16], F32)
    gs2 = sbT("gs2", [128, 16], F32)
    grs = sbT("grs", [128, 16], F32)
    qfqT = sbT("qfqT", [128, 2048], BF16)
    xnT = sbT("xnT", [128, 16 * T], BF16)
    wslab = [sbT("wslab0", [128, 8 * 512], BF16), sbT("wslab1", [128, 8 * 512], BF16)]
    xl0 = sbT("xl0", [128, D], F32)
    scr2 = sbT("scr2", [128, D], F32)
    scr3 = qnat
    dump = sbT("dump", [128, D], F32)
    stage = [sbT("stage0", [128, 512], BF16), sbT("stage1", [128, 512], BF16)]
    smalls = sbT("smalls", [128, NL0 * 288], F32)
    xh_l0 = sbT("xh_l0", [128, DIN], BF16)
    xsh_l0 = sbT("xsh_l0", [128, DIN], BF16)
    X_l0 = sbT("X_l0", [128, DIN], BF16)
    xd2 = [sbT("xd0", [128, 512], BF16), sbT("xd1", [128, 512], BF16)]
    dt_t = sbT("dt_t", [128, NH], F32)
    adisc = sbT("adisc", [128, NH], F32)
    gam_bf = sbT("gam_bf", [128, NH], BF16)
    bet_bf = sbT("bet_bf", [128, NH], BF16)
    lam_t = sbT("lam_t", [128, NH], F32)
    w64a = sbT("w64a", [128, NH], F32)
    w64b = sbT("w64b", [128, NH], F32)
    w64c = sbT("w64c", [128, NH], F32)
    bn_bf = sbT("bn_bf", [128, NST], BF16)
    cn_bf = sbT("cn_bf", [128, NST], BF16)
    th_t = sbT("th_t", [128, 32], F32)
    angles = sbT("angles", [128, 32], F32)
    sin_t = sbT("sin_t", [128, 32], F32)
    cos_t = sbT("cos_t", [128, 32], F32)
    tg_u = sbT("tg_u", [128, 32], F32)
    tg_k = sbT("tg_k", [128, 32], I32)
    tg_c = sbT("tg_c", [128, 32], F32)
    offT = sbT("offT", [32, 1], F32)
    thT = sbT("thT", [32, 128], F32)
    zeros32 = sbT("zeros32", [32, 128], F32)
    angT = sbT("angT", [32, 128], F32)
    acs = sbT("acs", [128, NH], F32)
    acsT = sbT("acsT", [64, 128], F32)
    ea_t = sbT("ea_t", [128, NH], F32)
    dectmp = sbT("dectmp", [64, 128], F32)
    dect = sbT("dect", [64, 128], BF16)
    dec_tm = sbT("dec_tm", [128, NH], BF16)
    decay_arr = sbT("decay_arr", [128, 64], F32)
    bnT = sbT("bnT", [64, 128], BF16)
    cnT_e = sbT("cnT_e", [128, 128], BF16)
    cnT_o = sbT("cnT_o", [128, 128], BF16)
    gu_bd = sbT("gu_bd", [128, 128], BF16)
    bd2 = sbT("bd2", [128, 2 * NH * NST], BF16)
    dsrc = sbT("dsrc", [128, 1024], F32)
    lpair = sbT("lpair", [128, 16 * 128], BF16)
    mbd = lpair
    s_st = sbT("s_st", [128, 2048], F32)
    s_bf2 = [sbT("s_bf0", [128, 256], BF16), sbT("s_bf1", [128, 256], BF16)]
    eab2 = [sbT("eab0", [128, 512], BF16), sbT("eab1", [128, 512], BF16)]
    ssq8 = sbT("ssq8", [128, NL0 * 8], F32)
    istd_t = sbT("istd_t", [128, NL0], F32)
    xs_t = sbT("xs_t", [128, NL0], F32)
    mean_t = sbT("mean_t", [128, 1], F32)
    m2_t = sbT("m2_t", [128, 1], F32)
    var_t = sbT("var_t", [128, 1], F32)
    ssq_t = sbT("ssq_t", [128, 1], F32)
    wgt = [sbT("wgt0", [128, D], BF16), sbT("wgt1", [128, D], BF16)]
    lhsb = [sbT("lhs0", [128, T], BF16), sbT("lhs1", [128, T], BF16)]

    P = [psT(f"p{i}", [128, 512], F32) for i in range(7)]
    psm = psT("psm", [128, 512], F32)
    ptr_bf = psm[:, 384:512].bitcast(BF16)

    sch = Sched(nc, es)
    OP, DMA = sch.op, sch.dma
    _mmb = [0]

    def npsw(e, base):
        if _mmb[0] != base:
            e.nop()
            _mmb[0] = base

    def ident_fill(t):
        OP("gpsimd", lambda e, t=t: e.memset(t[:], 0.0), writes=(t.name,))
        OP("gpsimd", lambda e, t=t: e.affine_select(out=t[:], in_=t[:],
           compare_op=A.not_equal, fill=1.0, base=0, pattern=[[-1, t.shape[1]]],
           channel_multiplier=1), reads=(t.name,), writes=(t.name,))

    ident_fill(ident_bf)
    ident_fill(ident_f)
    OP("gpsimd", lambda e: e.memset(cst[:, 0:1], LN_EPS), writes=("cst",))
    OP("gpsimd", lambda e: e.memset(cst[:, 1:2], RMS_EPS), reads=("cst",), writes=("cst",))
    OP("gpsimd", lambda e: e.memset(cst[:, 2:3], 1.0), reads=("cst",), writes=("cst",))
    OP("gpsimd", lambda e: e.memset(u2[:], 0.0), writes=("u2",))
    OP("gpsimd", lambda e: e.affine_select(out=u2[:], in_=u2[:], compare_op=A.is_gt,
       fill=1.0, base=0, pattern=[[-1, 128]], channel_multiplier=1),
       reads=("u2",), writes=("u2",))
    OP("gpsimd", lambda e: e.memset(u2[0:64, 64:128], 0.0), reads=("u2",), writes=("u2",))
    OP("gpsimd", lambda e: e.memset(u64[:], 0.0), writes=("u64",))
    OP("gpsimd", lambda e: e.affine_select(out=u64[:], in_=u64[:], compare_op=A.is_gt,
       fill=1.0, base=0, pattern=[[-1, 64]], channel_multiplier=1),
       reads=("u64",), writes=("u64",))
    OP("gpsimd", lambda e: e.memset(gu_bd[:], 0.0), writes=("gu_bd",))
    OP("gpsimd", lambda e: e.memset(lpair[:], 0.0), writes=("lpair",))
    OP("gpsimd", lambda e: e.memset(bd2[:], 0.0), writes=("bd2",))
    OP("gpsimd", lambda e: e.memset(cnT_e[:], 0.0), writes=("cnT_e",))
    OP("gpsimd", lambda e: e.memset(cnT_o[:], 0.0), writes=("cnT_o",))
    OP("gpsimd", lambda e: e.memset(zeros32[:], 0.0), writes=("zeros32",))
    OP("gpsimd", lambda e: e.memset(offT[:], 0.0), writes=("offT",))
    OP("gpsimd", lambda e: e.memset(s_st[:], 0.0), writes=tuple(f"s_st_{o}" for o in range(8)))
    OP("gpsimd", lambda e: e.memset(xsh_l0[:], 0.0), writes=("xsh_l0",))
    for _zc in range(8):
        DMA("gpsimd", lambda e, _zc=_zc: e.dma_start(xh_scr[0:1, _zc * 512:(_zc + 1) * 512],
            xsh_l0[0:1, 0:512]), reads=("xsh_l0",), writes=("xh_0_0",))

    def bc_load(dst_t, src_t, w):
        DMA("sync", lambda e, d=dst_t, s=src_t, w=w: e.dma_start(
            d[:, 0:w], bass.AP(s, 0, [[0, 128], [1, w]])), writes=(dst_t.name,))

    bc_load(dtb_bc, dt_bias_d, NH)
    bc_load(bb_bc, b_bias, NST)
    bc_load(cb_bc, c_bias, NST)
    bc_load(bias_bc, fb, D)
    DMA("sync", lambda e: e.dma_start(w64a[:, 0:NH],
        bass.AP(d_par, 0, [[0, 128], [1, NH]])), writes=("w64a",))
    OP("vector", lambda e: e.tensor_copy(dpar_bc[:], w64a[:, 0:NH]),
       reads=("w64a",), writes=("dpar_bc",))
    DMA("sync", lambda e: e.dma_start(arow[:], a_log[0:1, :]), writes=("arow",))
    OP("scalar", lambda e: e.activation(arow2[:], arow[:], AF.Exp, scale=1.0),
       reads=("arow",), writes=("arow2",))
    OP("vector", lambda e: e.tensor_scalar_mul(arow2[:], arow2[:], -1.0),
       reads=("arow2",), writes=("arow2",))
    DMA("gpsimd", lambda e: e.dma_start(bounce[0:1, 0:NH], arow2[:]),
        reads=("arow2",), writes=("bounce0",))
    DMA("sync", lambda e: e.dma_start(negae_bc[:],
        bass.AP(bounce, 0, [[0, 128], [1, NH]])), reads=("bounce0",), writes=("negae_bc",))

    QBUF = {"vector": (qnat, qu, qk, qfq, qgm, qs, qrs),
            "gpsimd": (gnat, gu2, gk, gfq, ggm, gs2, grs)}

    def quant_grouped(src_dmas, dst_t, dst_row, Pn, W, ngrp, gsz, qmax, res_out, dst_col=0, ve="vector"):
        qnat, qu, qk, qfq, qgm, qs, qrs = QBUF[ve]
        ve = "vector"
        off = float(qmax + 1)
        for fn in src_dmas:
            DMA("sync", fn, writes=(qnat.name,))
        OP("vector", lambda e, Pn=Pn, W=W, ng=ngrp:
           e.reduce_max(qgm[0:Pn, 0:ng].rearrange("p (g o) -> p g o", o=1),
                        qnat[0:Pn, 0:W].rearrange("p (g c) -> p g c", g=ng),
                        axis=AX, apply_absolute_value=True),
           reads=(qnat.name,), writes=(qgm.name,))
        OP(ve, lambda e, Pn=Pn, ng=ngrp, q=qmax, fl=(1e-10 if qmax == 7 else 1e-8):
           e.tensor_scalar(qs[0:Pn, 0:ng], qgm[0:Pn, 0:ng], fl, 1.0 / q, A.max, A.mult),
           reads=(qgm.name,), writes=(qs.name,))
        OP("vector", lambda e, Pn=Pn, ng=ngrp: e.reciprocal(qrs[0:Pn, 0:ng], qs[0:Pn, 0:ng]),
           reads=(qs.name,), writes=(qrs.name,))
        for g in range(ngrp):
            OP(ve, lambda e, Pn=Pn, g=g, gsz=gsz, off=off:
               e.tensor_scalar(qu[0:Pn, g * gsz:(g + 1) * gsz],
                               qnat[0:Pn, g * gsz:(g + 1) * gsz],
                               qrs[0:Pn, g:g + 1], off + 0.5, A.mult, A.add),
               reads=(qnat.name, qrs.name), writes=(qu.name,))
        OP(ve, lambda e, Pn=Pn, W=W: e.tensor_scalar(qk[0:Pn, 0:W], qu[0:Pn, 0:W],
           -0.5, None, A.add, A.bypass), reads=(qu.name,), writes=(qk.name,))
        OP(ve, lambda e, Pn=Pn, W=W: e.scalar_tensor_tensor(qu[0:Pn, 0:W],
           qu[0:Pn, 0:W], -1.0, qk[0:Pn, 0:W], A.add, A.is_ge),
           reads=(qu.name, qk.name), writes=(qu.name,))
        OP(ve, lambda e, Pn=Pn, W=W: e.tensor_tensor(qu[0:Pn, 0:W],
           qu[0:Pn, 0:W], qk[0:Pn, 0:W], A.add), reads=(qu.name, qk.name), writes=(qu.name,))
        OP(ve, lambda e, Pn=Pn, W=W, off=off: e.tensor_scalar(qu[0:Pn, 0:W],
           qu[0:Pn, 0:W], -off, None, A.add, A.bypass), reads=(qu.name,), writes=(qu.name,))
        OP(ve, lambda e, Pn=Pn, W=W, ng=ngrp, gsz=gsz:
           e.tensor_tensor(qfq[0:Pn, 0:W].rearrange("p (g c) -> p g c", g=ng),
                           qu[0:Pn, 0:W].rearrange("p (g c) -> p g c", g=ng),
                           ap_of(qs, 0, Pn, 0, [(1, ngrp), (0, gsz)]), A.mult),
           reads=(qu.name, qs.name), writes=(qfq.name,))
        nk = W // 128
        assert nk % 2 == 0
        for k2 in range(0, nk, 2):
            def trp(e, k2=k2, Pn=Pn):
                npsw(e, 0)
                e.transpose(ptr_bf[0:128, 0:Pn], qfq[0:Pn, k2 * 128:(k2 + 1) * 128],
                            ident_bf[0:Pn, 0:Pn])
                return e.transpose(ptr_bf[0:128, 128:128 + Pn],
                                   qfq[0:Pn, (k2 + 1) * 128:(k2 + 2) * 128],
                                   ident_bf[0:Pn, 0:Pn])
            OP("tensor", trp, reads=(qfq.name, "ident_bf"), writes=("psm",))
            def cpb(e, k2=k2, Pn=Pn):
                if Pn == 128:
                    return e.tensor_copy(qfqT[0:128, k2 * 128:(k2 + 2) * 128],
                                         ptr_bf[0:128, 0:256])
                e.tensor_copy(qfqT[0:128, k2 * Pn:(k2 + 1) * Pn], ptr_bf[0:128, 0:Pn])
                return e.tensor_copy(qfqT[0:128, (k2 + 1) * Pn:(k2 + 2) * Pn],
                                     ptr_bf[0:128, 128:128 + Pn])
            OP("vector", cpb, reads=("psm",), writes=(qfqT.name,))
        rowlen = int(np.prod(dst_t.shape[1:]))
        return DMA("gpsimd", lambda e, r=dst_row, Pn=Pn, W=W, d=dst_t, dc=dst_col, rl=rowlen:
                   e.dma_start(
                       bass.AP(d, dc * rl + r, [[rl, 128], [128 * rl, W // 128], [1, Pn]]),
                       bass.AP(qfqT, 0, [[qfqT.shape[1], 128], [Pn, W // 128], [1, Pn]])),
                   reads=(qfqT.name,), writes=(res_out,))

    def gen_quant_win(jobs, toks):
        for (r, n, srcs) in jobs:
            dmas = []
            for (st, s0, d0, cnt) in srcs:
                dmas.append(lambda e, st=st, a=s0, ro=d0, c=cnt:
                            e.dma_start(qnat[ro:ro + c, 0:D], st[a:a + c, 0:D]))
            tok = quant_grouped(dmas, wq_in, r, n, D, 16, 128, 7, f"wqin_{r}")
            toks["wq_in"].append((r, n, tok))

    def win_jobs(lo, hi):
        segs = [(w_in, 8192, R_SM, 192), (w_th, 0, R_SM + 192, 32),
                (w_lm, 0, R_SM + 224, 64), (w_in, 4096, R_XC, 4096), (w_in, 0, R_Z, 4096)]
        jobs = []
        r = lo
        while r < hi:
            n = min(128, hi - r)
            srcs = []
            for (st, s0, d0, cnt) in segs:
                a, b = max(r, d0), min(r + n, d0 + cnt)
                if a < b:
                    srcs.append((st, s0 + a - d0, a - r, b - a))
            jobs.append((r, n, srcs))
            r += n
        return jobs

    def gen_quant_wout(toks):
        for j in range(16):
            for hf in range(2):
                tok = quant_grouped(
                    [lambda e, j=j, hf=hf: e.dma_start(qnat[0:128, 0:D],
                        w_out[j * 128:(j + 1) * 128, hf * D:(hf + 1) * D])],
                    wq_out, j * 128, 128, D, 16, 128, 7, f"wqout_{j}_{hf}",
                    dst_col=hf * D, ve="vector")
                toks["wq_out"].append((j * 128, 128, tok))

    def gen_quant_ffn(toks):
        for j in range(16):
            for hf in range(2):
                DMA("sync", lambda e, j=j, hf=hf: e.dma_start(gnat[0:128, 0:1024],
                    fw[j * 128:(j + 1) * 128, hf * 1024:(hf + 1) * 1024]), writes=("gnat",))
                if hf == 0:
                    OP("vector", lambda e: e.reduce_max(ggm[:, 0:1], gnat[:, 0:1024], axis=AX,
                       apply_absolute_value=True), reads=("gnat",), writes=("ggm",))
                else:
                    OP("vector", lambda e: e.reduce_max(ggm[:, 1:2], gnat[:, 0:1024], axis=AX,
                       apply_absolute_value=True), reads=("gnat",), writes=("ggm",))
            OP("vector", lambda e: e.tensor_tensor(ggm[:, 0:1], ggm[:, 0:1], ggm[:, 1:2],
               A.max), reads=("ggm",), writes=("ggm",))
            OP("vector", lambda e: e.tensor_scalar(gs2[:, 0:1], ggm[:, 0:1], 1e-8,
               1.0 / 127.0, A.max, A.mult), reads=("ggm",), writes=("gs2",))
            DMA("gpsimd", lambda e, j=j: e.dma_start(bounce[1:2, j * 128:(j + 1) * 128],
                gs2[:, 0:1]), reads=("gs2",), writes=(f"wsrow_{j}",))
            OP("vector", lambda e: e.reciprocal(grs[:, 0:1], gs2[:, 0:1]),
               reads=("gs2",), writes=("grs",))
            for hf in range(2):
                DMA("sync", lambda e, j=j, hf=hf: e.dma_start(gnat[0:128, 0:1024],
                    fw[j * 128:(j + 1) * 128, hf * 1024:(hf + 1) * 1024]), writes=("gnat",))
                OP("vector", lambda e: e.tensor_scalar(gu2[:, 0:1024], gnat[:, 0:1024],
                   grs[:, 0:1], 128.5, A.mult, A.add), reads=("gnat", "grs"), writes=("gu2",))
                OP("vector", lambda e: e.tensor_scalar(gk[:, 0:1024], gu2[:, 0:1024], -0.5,
                   None, A.add, A.bypass), reads=("gu2",), writes=("gk",))
                OP("vector", lambda e: e.scalar_tensor_tensor(gu2[:, 0:1024], gu2[:, 0:1024],
                   -1.0, gk[:, 0:1024], A.add, A.is_ge), reads=("gu2", "gk"), writes=("gu2",))
                OP("vector", lambda e: e.tensor_tensor(gu2[:, 0:1024], gu2[:, 0:1024],
                   gk[:, 0:1024], A.add), reads=("gu2", "gk"), writes=("gu2",))
                OP("vector", lambda e: e.tensor_scalar(gfq[:, 0:1024], gu2[:, 0:1024],
                   -128.0, None, A.add, A.bypass), reads=("gu2",), writes=("gfq",))
                for k2 in range(0, 8, 2):
                    def trpf(e, k2=k2):
                        npsw(e, 0)
                        e.transpose(ptr_bf[0:128, 0:128], gfq[:, k2 * 128:(k2 + 1) * 128],
                                    ident_bf[:])
                        return e.transpose(ptr_bf[0:128, 128:256],
                                           gfq[:, (k2 + 1) * 128:(k2 + 2) * 128], ident_bf[:])
                    OP("tensor", trpf, reads=("gfq", "ident_bf"), writes=("psm",))
                    OP("vector", lambda e, k2=k2: e.tensor_copy(
                       qfqT[0:128, k2 * 128:(k2 + 2) * 128], ptr_bf[0:128, 0:256]),
                       reads=("psm",), writes=(qfqT.name,))
                tok = DMA("gpsimd", lambda e, j=j, hf=hf: e.dma_start(
                    bass.AP(wq_ffn, hf * 1024 * D + j * 128, [[D, 128], [128 * D, 8], [1, 128]]),
                    bass.AP(qfqT, 0, [[qfqT.shape[1], 128], [128, 8], [1, 128]])),
                    reads=(qfqT.name,), writes=(f"wqffn_{j}",))
            toks["wq_ffn"].append((j * 128, 128, tok))
        DMA("sync", lambda e: e.dma_start(ws_bc[:],
            bass.AP(bounce, 4096, [[0, 128], [1, D]])), writes=("ws_bc",),
            extra=[t for (_, _, t) in toks["wq_ffn"]]
                  + [sch.last_w[f"wsrow_{j}"] for j in range(16)])

    def rows_dep(toks, lo, hi):
        return [t for (r, n, t) in toks if r < hi and r + n > lo]

    def istd_pipe(v_res, v_ap, o_ap, o_res, eps_col, scale):
        OP("scalar", lambda e: e.activation(o_ap, v_ap, AF.Ln,
           bias=cst[:, eps_col:eps_col + 1], scale=scale),
           reads=(v_res, "cst"), writes=(o_res,))
        OP("scalar", lambda e: e.activation(o_ap, o_ap, AF.Exp, scale=-0.5),
           reads=(o_res,), writes=(o_res,))

    def ln_stats(src_ap, src_res, extra_reads=()):
        OP("vector", lambda e: e.reduce_sum(mean_t[:], src_ap, axis=AX),
           reads=(src_res,) + tuple(extra_reads), writes=("mean_t",))
        OP("vector", lambda e: e.tensor_scalar_mul(mean_t[:], mean_t[:], 1.0 / D),
           reads=("mean_t",), writes=("mean_t",))
        OP("scalar", lambda e: e.activation(dump[:, 0:D], src_ap, AF.Square,
           accum_out=ssq_t[:]), reads=(src_res,), writes=("dump", "ssq_t"))
        OP("vector", lambda e: e.tensor_tensor(m2_t[:], mean_t[:], mean_t[:], A.mult),
           reads=("mean_t",), writes=("m2_t",))
        OP("vector", lambda e: e.scalar_tensor_tensor(m2_t[:], ssq_t[:], 1.0 / D, m2_t[:],
           A.mult, A.subtract), reads=("ssq_t", "m2_t"), writes=("m2_t",))
        istd_pipe("m2_t", m2_t[:], var_t[:], "var_t", 0, 1.0)

    def gen_ln_x(t):
        for p in range(NL0):
            g0 = t * T + p * 128
            DMA("sync", lambda e, g0=g0: e.dma_start(xl0[:], x_in[g0:g0 + 128, :]),
                writes=("xl0",))
            ln_stats(xl0[:, 0:D], "xl0")
            for cq in range(4):
                st = stage[cq % 2]
                OP("vector", lambda e, cq=cq, st=st: e.tensor_scalar(st[:, :],
                   xl0[:, cq * 512:(cq + 1) * 512], mean_t[:], var_t[:],
                   A.subtract, A.mult), reads=("xl0", "mean_t", "var_t"),
                   writes=(st.name,))
                DMA("gpsimd", lambda e, g0=g0, cq=cq, st=st: e.dma_start(
                    xn_scr[g0:g0 + 128, cq * 512:(cq + 1) * 512], st[:, :]),
                    reads=(st.name,), writes=(f"xn_{t}_{p}",))

    def gen_xnT(t):
        for k in range(16):
            DMA("sync", lambda e, k=k, t=t: e.dma_start_transpose(
                xnT[:, k * T:(k + 1) * T],
                xn_scr[t * T:(t + 1) * T, k * 128:(k + 1) * 128]),
                writes=("xnT",),
                extra=[sch.last_w[f"xn_{t}_{p}"] for p in range(NL0)])

    def gen_inproj(t, qtoks, fcs=None):
        for fc, (rowoff, W, kindname) in enumerate(FC):
            if fcs is not None and fc not in fcs:
                continue
            for kb in range(2):
                slab = wslab[kb]
                DMA("sync", lambda e, kb=kb, ro=rowoff, W=W, sl=slab: e.dma_start(
                    bass.AP(sl, 0, [[sl.shape[1], 128], [512, 8], [1, W]]),
                    bass.AP(wq_in, kb * 1024 * NPROJ + ro, [[NPROJ, 128], [128 * NPROJ, 8], [1, W]])),
                    writes=(slab.name,),
                    extra=rows_dep(qtoks["wq_in"], rowoff, rowoff + W))
            for p in range(NL0):
                pa = P[6] if (fc * NL0 + p) % 2 == 0 else psm
                def mmf(e, pa=pa, W=W, p=p):
                    inst = None
                    npsw(e, 0)
                    for kk in range(16):
                        kb, k = kk // 8, kk % 8
                        inst = e.matmul(pa[:, 0:W],
                                        xnT[:, kk * T + p * 128:kk * T + (p + 1) * 128],
                                        wslab[kb][:, k * 512:k * 512 + W],
                                        start=(kk == 0), stop=(kk == 15))
                    return inst
                OP("tensor", mmf, reads=("xnT", "wslab0", "wslab1"), writes=(pa.name,))
                if kindname == "sm":
                    OP("vector", lambda e, pa=pa, p=p: e.tensor_copy(
                       smalls[:, p * 288:(p + 1) * 288], pa[:, 0:288]),
                       reads=(pa.name,), writes=(f"smalls_{p}",))
                else:
                    st = stage[(fc + p) % 2]
                    OP("scalar", lambda e, pa=pa, st=st: e.activation(st[:, :],
                       pa[:, 0:512], AF.Silu), reads=(pa.name,), writes=(st.name,))
                    dst = xh_scr if kindname == "xc" else sz_scr
                    coff = (fc - 1) * 512 if kindname == "xc" else (fc - 9) * 512
                    res = f"xh_{t}_{p}" if kindname == "xc" else f"sz_{t}_{p}"
                    DMA("gpsimd", lambda e, g0=t * T + p * 128, co=coff, st=st, d=dst,
                        ro=(1 if kindname == "xc" else 0):
                        e.dma_start(d[g0 + ro:g0 + ro + 128, co:co + 512], st[:, :]),
                        reads=(st.name,), writes=(res,))

    def gen_smalls(t, p):
        so = p * 288
        smr = f"smalls_{p}"
        OP("vector", lambda e: e.tensor_tensor(w64a[:], smalls[:, so + S_DT:so + S_DT + 64],
           dtb_bc[:], A.add), reads=(smr, "dtb_bc"), writes=("w64a",))
        OP("vector", lambda e: e.scalar_tensor_tensor(w64b[:], w64a[:], -1.0, w64a[:],
           A.mult, A.min), reads=("w64a",), writes=("w64b",))
        OP("scalar", lambda e: e.activation(w64b[:], w64b[:], AF.Exp),
           reads=("w64b",), writes=("w64b",))
        OP("scalar", lambda e: e.activation(w64b[:], w64b[:], AF.Ln,
           bias=cst[:, 2:3]), reads=("w64b", "cst"), writes=("w64b",))
        OP("vector", lambda e: e.scalar_tensor_tensor(dt_t[:], w64a[:], 0.0, w64b[:],
           A.max, A.add), reads=("w64a", "w64b"), writes=("dt_t",))
        OP("vector", lambda e: e.tensor_tensor(adisc[:], dt_t[:], negae_bc[:], A.mult),
           reads=("dt_t", "negae_bc"), writes=("adisc",))
        OP("scalar", lambda e: e.activation(lam_t[:], smalls[:, so + S_LAM:so + S_LAM + 64],
           AF.Sigmoid), reads=(smr,), writes=("lam_t",))
        OP("vector", lambda e: e.tensor_tensor(w64a[:], lam_t[:], dt_t[:], A.mult),
           reads=("lam_t", "dt_t"), writes=("w64a",))
        OP("vector", lambda e: e.tensor_copy(gam_bf[:], w64a[:]),
           reads=("w64a",), writes=("gam_bf",))
        OP("scalar", lambda e: e.activation(w64b[:], adisc[:], AF.Exp),
           reads=("adisc",), writes=("w64b",))
        OP("vector", lambda e: e.tensor_scalar(w64c[:], lam_t[:], -1.0, -1.0,
           A.add, A.mult), reads=("lam_t",), writes=("w64c",))
        OP("vector", lambda e: e.tensor_tensor(w64c[:], w64c[:], dt_t[:], A.mult),
           reads=("w64c", "dt_t"), writes=("w64c",))
        OP("vector", lambda e: e.tensor_tensor(w64c[:], w64c[:], w64b[:], A.mult),
           reads=("w64c", "w64b"), writes=("w64c",))
        OP("vector", lambda e: e.tensor_copy(bet_bf[:], w64c[:]),
           reads=("w64c",), writes=("bet_bf",))
        OP("vector", lambda e: e.reduce_sum(m2_t[:], dt_t[:], axis=AX),
           reads=("dt_t",), writes=("m2_t",))
        OP("vector", lambda e: e.tensor_scalar_mul(m2_t[:], m2_t[:], 1.0 / NH),
           reads=("m2_t",), writes=("m2_t",))
        OP("vector", lambda e: e.tensor_scalar_mul(th_t[:], smalls[:, so + S_TH:so + S_TH + 32],
           m2_t[:]), reads=(smr, "m2_t"), writes=("th_t",))
        OP("tensor", lambda e: (npsw(e, 0), e.transpose(psm[0:32, 0:128], th_t[:],
           ident_f[:]))[1], reads=("th_t", "ident_f"), writes=("psm",))
        OP("vector", lambda e: e.tensor_copy(thT[:], psm[0:32, 0:128]),
           reads=("psm",), writes=("thT",))
        OP("vector", lambda e: e.tensor_tensor_scan(angT[:], thT[:], zeros32[:],
           offT[:, 0:1], A.add, A.add), reads=("thT", "zeros32", "offT"), writes=("angT",))
        OP("vector", lambda e: e.tensor_copy(offT[:, 0:1], angT[:, 127:128]),
           reads=("angT",), writes=("offT",))
        OP("tensor", lambda e: (npsw(e, 0), e.transpose(psm[0:128, 128:160], angT[:],
           ident_f[0:32, 0:32]))[1], reads=("angT", "ident_f"), writes=("psm",))
        OP("vector", lambda e: e.tensor_copy(angles[:], psm[0:128, 128:160]),
           reads=("psm",), writes=("angles",))
        def trig(dst_t, quarter):
            OP("vector", lambda e, q=quarter: e.tensor_scalar(tg_u[:], angles[:],
               1.0 / TWO_PI, 256.5 + q, A.mult, A.add), reads=("angles",), writes=("tg_u",))
            OP("vector", lambda e: e.tensor_scalar(tg_k[:], tg_u[:], -0.5, None,
               A.add, A.bypass), reads=("tg_u",), writes=("tg_k",))
            OP("vector", lambda e: e.scalar_tensor_tensor(tg_c[:], tg_u[:], -1.0,
               tg_k[:], A.add, A.is_ge), reads=("tg_u", "tg_k"), writes=("tg_c",))
            OP("vector", lambda e: e.tensor_tensor(tg_c[:], tg_c[:], tg_k[:], A.add),
               reads=("tg_c", "tg_k"), writes=("tg_c",))
            OP("vector", lambda e: e.tensor_tensor(tg_c[:], tg_u[:], tg_c[:], A.subtract),
               reads=("tg_u", "tg_c"), writes=("tg_c",))
            OP("vector", lambda e: e.tensor_scalar(tg_c[:], tg_c[:], -0.5, TWO_PI,
               A.add, A.mult), reads=("tg_c",), writes=("tg_c",))
            OP("vector", lambda e: e.tensor_scalar(tg_c[:], tg_c[:], -PI_LO, PI_LO,
               A.max, A.min), reads=("tg_c",), writes=("tg_c",))
            OP("scalar", lambda e, d=dst_t: e.activation(d[:], tg_c[:], AF.Sin),
               reads=("tg_c",), writes=(dst_t.name,))
        trig(sin_t, 0.0)
        trig(cos_t, 0.25)
        if "trig" in taps:
            OP("vector", lambda e: e.tensor_copy(dump[:, 0:32], sin_t[:]), reads=("sin_t",), writes=("dump",))
            OP("vector", lambda e: e.tensor_copy(dump[:, 32:64], cos_t[:]), reads=("cos_t",), writes=("dump",))
            if t == 0 and p == 0:
                DMA("gpsimd", lambda e: e.dma_start(taps["trig"][:, :], dump[:, 0:64]), reads=("dump",), writes=("tapd",))
        def bc_norm(src_off, bias_t, out_bf):
            OP("scalar", lambda e: e.activation(dump[:, 0:64],
               smalls[:, so + src_off:so + src_off + 64], AF.Square, accum_out=ssq_t[:]),
               reads=(smr,), writes=("dump", "ssq_t"))
            istd_pipe("ssq_t", ssq_t[:], var_t[:], "var_t", 1, 1.0 / NST)
            OP("vector", lambda e, bt=bias_t: e.scalar_tensor_tensor(w64a[:],
               smalls[:, so + src_off:so + src_off + 64], var_t[:], bt[:],
               A.mult, A.add), reads=(smr, "var_t", bias_t.name), writes=("w64a",))
            OP("vector", lambda e: e.tensor_tensor(w64b[:, 0:32], w64a[:, 0:32],
               cos_t[:], A.mult), reads=("w64a", "cos_t"), writes=("w64b",))
            OP("vector", lambda e: e.tensor_tensor(w64c[:, 0:32], w64a[:, 32:64],
               sin_t[:], A.mult), reads=("w64a", "sin_t"), writes=("w64c",))
            OP("vector", lambda e, o=out_bf: e.tensor_tensor(o[:, 0:32], w64b[:, 0:32],
               w64c[:, 0:32], A.subtract), reads=("w64b", "w64c"), writes=(out_bf.name,))
            OP("vector", lambda e: e.tensor_tensor(w64b[:, 0:32], w64a[:, 0:32],
               sin_t[:], A.mult), reads=("w64a", "sin_t"), writes=("w64b",))
            OP("vector", lambda e: e.tensor_tensor(w64c[:, 0:32], w64a[:, 32:64],
               cos_t[:], A.mult), reads=("w64a", "cos_t"), writes=("w64c",))
            OP("vector", lambda e, o=out_bf: e.tensor_tensor(o[:, 32:64], w64b[:, 0:32],
               w64c[:, 0:32], A.add), reads=("w64b", "w64c"), writes=(out_bf.name,))
        bc_norm(S_B, bb_bc, bn_bf)
        bc_norm(S_C, cb_bc, cn_bf)
        if "smalls" in taps and t == 0:
            DMA("gpsimd", lambda e, p=p: e.dma_start(taps["smalls"][p * 128:(p + 1) * 128, :],
                smalls[:, so:so + 288]), reads=(smr,), writes=("tapd",))

    def gen_ssd(t, p, qtoks, ssd_upto=99):
        g0 = t * T + p * 128
        DMA("sync", lambda e: e.dma_start(xh_l0[:], xh_scr[g0 + 1:g0 + 129, :]),
            writes=("xh_l0",), extra=[sch.last_w[f"xh_{t}_{p}"]])
        prev = (f"xh_{t}_{p - 1}" if p > 0 else
                (f"xh_{t - 1}_{NL0 - 1}" if t > 0 else "xh_0_0"))
        DMA("sync", lambda e: e.dma_start(xsh_l0[:, :], xh_scr[g0:g0 + 128, :]),
            writes=("xsh_l0",), extra=[sch.last_w[f"xh_{t}_{p}"], sch.last_w[prev]])
        OP("vector", lambda e: e.tensor_tensor(
           X_l0[:].rearrange("q (h c) -> q h c", h=NH),
           xh_l0[:].rearrange("q (h c) -> q h c", h=NH),
           ap_of(gam_bf, 0, 128, 0, [(1, NH), (0, HD)]), A.mult),
           reads=("xh_l0", "gam_bf"), writes=("X_l0",))
        OP("vector", lambda e: e.tensor_tensor(
           xsh_l0[:].rearrange("q (h c) -> q h c", h=NH),
           xsh_l0[:].rearrange("q (h c) -> q h c", h=NH),
           ap_of(bet_bf, 0, 128, 0, [(1, NH), (0, HD)]), A.mult),
           reads=("xsh_l0", "bet_bf"), writes=("xsh_l0",))
        OP("vector", lambda e: e.tensor_tensor(X_l0[:], X_l0[:], xsh_l0[:], A.add),
           reads=("X_l0", "xsh_l0"), writes=("X_l0",))
        DMA("sync", lambda e: e.dma_start(xsh_l0[:], sz_scr[g0:g0 + 128, :]),
            writes=("xsh_l0",), extra=[sch.last_w[f"sz_{t}_{p}"]])
        if "X" in taps and t == 0 and p == 0:
            DMA("gpsimd", lambda e: e.dma_start(taps["X"][:, :], X_l0[:]), reads=("X_l0",), writes=("tapd",))
        OP("tensor", lambda e: (npsw(e, 0), e.matmul(psm[:, 0:64], u2[:], adisc[:],
           start=True, stop=True))[1], reads=("u2", "adisc"), writes=("psm",))
        OP("vector", lambda e: e.tensor_copy(acs[:], psm[:, 0:64]),
           reads=("psm",), writes=("acs",))
        OP("scalar", lambda e: e.activation(ea_t[:], acs[:], AF.Exp),
           reads=("acs",), writes=("ea_t",))
        OP("tensor", lambda e: (npsw(e, 0), e.transpose(psm[0:64, 128:256], acs[:],
           ident_f[:]))[1], reads=("acs", "ident_f"), writes=("psm",))
        OP("vector", lambda e: e.tensor_copy(acsT[:], psm[0:64, 128:256]),
           reads=("psm",), writes=("acsT",))
        if "acs" in taps and t == 0 and p == 0:
            DMA("gpsimd", lambda e: e.dma_start(taps["acs"][:, :], acs[:]), reads=("acs",), writes=("tapd",))
        OP("tensor", lambda e: (npsw(e, 0), e.transpose(ptr_bf[0:64, 64:192],
           bn_bf[:, :], ident_bf[:]))[1], reads=("bn_bf", "ident_bf"), writes=("psm",))
        OP("vector", lambda e: e.tensor_copy(bnT[:, :], ptr_bf[0:64, 64:192]),
           reads=("psm",), writes=("bnT",))
        OP("tensor", lambda e: (npsw(e, 0), e.transpose(ptr_bf[0:64, 64:192],
           cn_bf[:, :], ident_bf[:]))[1], reads=("cn_bf", "ident_bf"), writes=("psm",))
        OP("vector", lambda e: e.tensor_copy(cnT_e[0:64, :], ptr_bf[0:64, 64:192]),
           reads=("psm",), writes=("cnT_e",))
        OP("tensor", lambda e: (npsw(e, 0), e.transpose(ptr_bf[64:128, 64:192],
           cn_bf[:, :], ident_bf[:]))[1], reads=("cn_bf", "ident_bf"), writes=("psm",))
        OP("vector", lambda e: e.tensor_copy(cnT_o[64:128, :], ptr_bf[64:128, 64:192]),
           reads=("psm",), writes=("cnT_o",))
        for c in range(2):
            cs = c * 64
            OP("vector", lambda e, cs=cs: e.tensor_scalar(dectmp[:, cs:cs + 64],
               acsT[:, cs:cs + 64], acsT[:, cs + 63:cs + 64], None, A.subtract, A.bypass),
               reads=("acsT",), writes=("dectmp",))
            OP("scalar", lambda e, cs=cs: e.activation(dect[:, cs:cs + 64],
               dectmp[:, cs:cs + 64], AF.Exp, scale=-1.0), reads=("dectmp",), writes=("dect",))
            OP("tensor", lambda e, cs=cs: (npsw(e, 0), e.transpose(ptr_bf[cs:cs + 64, 0:64],
               dect[:, cs:cs + 64], ident_bf[0:64, 0:64]))[1],
               reads=("dect", "ident_bf"), writes=("psm",))
            OP("vector", lambda e, cs=cs: e.tensor_copy(dec_tm[cs:cs + 64, :],
               ptr_bf[cs:cs + 64, 0:64]), reads=("psm",), writes=("dec_tm",))
            OP("tensor", lambda e, cs=cs: (npsw(e, 0), e.matmul(psm[cs:cs + 64, 256:320],
               bnT[:, cs:cs + 64], cnT_e[0:64, cs:cs + 64], start=True, stop=True))[1],
               reads=("bnT", "cnT_e"), writes=("psm",))
            OP("vector", lambda e, cs=cs: e.tensor_tensor(gu_bd[cs:cs + 64, cs:cs + 64],
               psm[cs:cs + 64, 256:320], u64[:], A.mult),
               reads=("psm", "u64"), writes=("gu_bd",))
            OP("vector", lambda e, cs=cs, c=c: e.tensor_tensor(
               ap_of(bd2, cs, 64, c * 4096, [(64, NH), (1, NST)]),
               ap_of(bn_bf, cs, 64, 0, [(0, NH), (1, NST)]),
               ap_of(dec_tm, cs, 64, 0, [(1, NH), (0, NST)]), A.mult),
               reads=("bn_bf", "dec_tm"), writes=("bd2",))

        if ssd_upto < 3:
            return
        DMA("gpsimd", lambda e: e.dma_start(bounce[2:3, 0:NH], ea_t[63:64, :]),
            reads=("ea_t",), writes=("dkrow0",))
        DMA("gpsimd", lambda e: e.dma_start(bounce[3:4, 0:NH], ea_t[127:128, :]),
            reads=("ea_t",), writes=("dkrow1",))
        for c in range(2):
            for par in range(2):
                DMA("sync", lambda e, c=c, par=par: e.dma_start(
                    decay_arr[par * 64:(par + 1) * 64, c * 32:(c + 1) * 32],
                    bass.AP(bounce, (2 + c) * 4096 + par, [[0, 64], [2, 32]]),
                    allow_slow_non_contiguous=True),
                    reads=(f"dkrow{c}",), writes=("decay_arr",))
        DMA("gpsimd", lambda e: e.dma_start(
            bass.AP(bounce, 8 * 4096, [[64, 64], [1, 64]]),
            acsT[:, 0:64]), reads=("acsT",), writes=("acsf0",))
        DMA("gpsimd", lambda e: e.dma_start(
            bass.AP(bounce, 9 * 4096, [[64, 64], [1, 64]]),
            acsT[:, 64:128]), reads=("acsT",), writes=("acsf1",))
        if ssd_upto < 4:
            return
        for q in range(4):
            h0 = q * 16
            for c in range(2):
                DMA("sync", lambda e, c=c, h0=h0: e.dma_start(dsrc[c * 64:(c + 1) * 64, 0:1024],
                    bass.AP(bounce, (8 + c) * 4096 + h0 * 64, [[0, 64], [1, 1024]])),
                    reads=(f"acsf{c}",), writes=("dsrc",))
                OP("vector", lambda e, c=c, h0=h0: e.tensor_tensor(
                   dsrc[c * 64:(c + 1) * 64, 0:1024].rearrange("q2 (h l) -> q2 h l", h=16),
                   dsrc[c * 64:(c + 1) * 64, 0:1024].rearrange("q2 (h l) -> q2 h l", h=16),
                   ap_of(acs, c * 64, 64, h0, [(1, 16), (0, 64)]), A.subtract),
                   reads=("dsrc", "acs"), writes=("dsrc",))
                OP("scalar", lambda e, c=c: e.activation(
                   ap_of(lpair, c * 64, 64, c * 64, [(128, 16), (1, 64)]),
                   dsrc[c * 64:(c + 1) * 64, 0:1024], AF.Exp), reads=("dsrc",), writes=("lpair",))
            OP("vector", lambda e: e.tensor_tensor(
               lpair[:].rearrange("q2 (h l) -> q2 h l", h=16),
               lpair[:].rearrange("q2 (h l) -> q2 h l", h=16),
               ap_of(gu_bd, 0, 128, 0, [(0, 16), (1, 128)]), A.mult),
               reads=("lpair", "gu_bd"), writes=("lpair",))
            if ssd_upto < 5:
                continue
            for oo in range(2):
                gen_oct(t, p, q * 2 + oo)
        if "sst" in taps and t == 0 and p == 1:
            DMA("gpsimd", lambda e: e.dma_start(taps["sst"][:, :], s_st[:]), reads=("s_st",), writes=("tapd",))

    def gen_oct(t, p, oct_i, ou=99):
        g0 = t * T + p * 128
        hbase = oct_i * 8
        oc = hbase * HD
        socol = oct_i * 256
        par2 = oct_i % 2
        pY, pO, pS = (P[0], P[1], P[2]) if par2 == 0 else (P[3], P[4], P[5])
        s_bf = s_bf2[par2]
        eab = eab2[par2]
        xd512 = xd2[par2]
        sres = f"s_st_{oct_i}"
        OP("vector", lambda e: e.tensor_copy(s_bf[:, :], s_st[:, socol:socol + 256]),
           reads=(sres,), writes=(s_bf.name,))
        OP("vector", lambda e: e.tensor_tensor(
           xd512[:].rearrange("q (h c) -> q h c", h=8),
           xh_l0[:, oc:oc + 512].rearrange("q (h c) -> q h c", h=8),
           ap_of(dpar_bc, 0, 128, hbase, [(1, 8), (0, HD)]), A.mult),
           reads=("xh_l0", "dpar_bc"), writes=(xd512.name,))
        OP("tensor", lambda e: (npsw(e, 0), e.matmul(pY[:, 0:512], ident_bf[:],
           xd512[:, 0:512], start=True, stop=False))[1],
           reads=("ident_bf", xd512.name), writes=(pY.name,))
        def ydiag(e):
            inst = None
            npsw(e, 0)
            for j in range(8):
                h = hbase + j
                inst = e.matmul(pY[:, j * 64:(j + 1) * 64],
                                lpair[:, (h % 16) * 128:(h % 16 + 1) * 128],
                                X_l0[:, h * HD:(h + 1) * HD],
                                start=False, stop=False)
            return inst
        OP("tensor", ydiag, reads=("lpair", "X_l0"), writes=(pY.name,))
        def close_py(e):
            npsw(e, 0)
            return e.matmul(pY[:, 0:512], ident_bf[:], xd512[:, 0:512], start=False, stop=True)
        if ou < 2:
            OP("tensor", close_py, reads=("ident_bf", xd512.name), writes=(pY.name,))
            return
        def yoff(e, c, start):
            inst = None
            npsw(e, 0)
            for j in range(8):
                h = hbase + j
                ct2 = cnT_e if h % 2 == 0 else cnT_o
                h2 = j // 2
                inst = e.matmul(pO[c * 64:c * 64 + 64, j * 64:(j + 1) * 64],
                                ct2[:, c * 64:c * 64 + 64],
                                s_bf[:, h2 * 64:(h2 + 1) * 64],
                                start=True, stop=True)
            return inst
        def states(e, c, start):
            inst = None
            npsw(e, 0)
            for j in range(8):
                h = hbase + j
                par = (h % 2) * 64
                h2 = j // 2
                inst = e.matmul(pS[par:par + 64, c * 256 + h2 * 64:c * 256 + (h2 + 1) * 64],
                                bd2[:, c * 4096 + h * HD:c * 4096 + (h + 1) * HD],
                                X_l0[:, h * HD:(h + 1) * HD],
                                start=True, stop=True)
            return inst
        OP("tensor", lambda e: yoff(e, 0, True), reads=("cnT_e", "cnT_o", s_bf.name), writes=(pO.name,))
        if ou < 3:
            OP("tensor", close_py, reads=("ident_bf", xd512.name), writes=(pY.name,))
            return
        OP("tensor", lambda e: states(e, 0, True), reads=("bd2", "X_l0"), writes=(pS.name,))
        OP("vector", lambda e: e.tensor_tensor(
           s_st[:, socol:socol + 256].rearrange("q (h2 c) -> q h2 c", h2=4),
           s_st[:, socol:socol + 256].rearrange("q (h2 c) -> q h2 c", h2=4),
           ap_of(decay_arr, 0, 128, 0 * 32 + oct_i * 4, [(1, 4), (0, 64)]), A.mult),
           reads=(sres, "decay_arr"), writes=(sres,))
        OP("vector", lambda e: e.tensor_tensor(s_st[:, socol:socol + 256],
           s_st[:, socol:socol + 256], pS[:, 0:256], A.add),
           reads=(sres, pS.name), writes=(sres,))
        OP("vector", lambda e: e.tensor_copy(s_bf[:, :], s_st[:, socol:socol + 256]),
           reads=(sres,), writes=(s_bf.name,))
        if ou < 4:
            OP("tensor", close_py, reads=("ident_bf", xd512.name), writes=(pY.name,))
            return
        OP("tensor", lambda e: yoff(e, 1, False), reads=("cnT_e", "cnT_o", s_bf.name), writes=(pO.name,))
        OP("tensor", lambda e: states(e, 1, False), reads=("bd2", "X_l0"), writes=(pS.name,))
        OP("vector", lambda e: e.tensor_tensor(
           s_st[:, socol:socol + 256].rearrange("q (h2 c) -> q h2 c", h2=4),
           s_st[:, socol:socol + 256].rearrange("q (h2 c) -> q h2 c", h2=4),
           ap_of(decay_arr, 0, 128, 1 * 32 + oct_i * 4, [(1, 4), (0, 64)]), A.mult),
           reads=(sres, "decay_arr"), writes=(sres,))
        OP("vector", lambda e: e.tensor_tensor(s_st[:, socol:socol + 256],
           s_st[:, socol:socol + 256], pS[:, 256:512], A.add),
           reads=(sres, pS.name), writes=(sres,))
        if ou < 5:
            OP("tensor", close_py, reads=("ident_bf", xd512.name), writes=(pY.name,))
            return
        OP("vector", lambda e: e.tensor_tensor(
           eab[:].rearrange("q (h c) -> q h c", h=8),
           pO[:].rearrange("q (h c) -> q h c", h=8),
           ap_of(ea_t, 0, 128, hbase, [(1, 8), (0, 64)]), A.mult),
           reads=(pO.name, "ea_t"), writes=(eab.name,))
        OP("tensor", lambda e: (npsw(e, 0), e.matmul(pY[:, 0:512], ident_bf[:], eab[:],
           start=False, stop=True))[1], reads=("ident_bf", eab.name), writes=(pY.name,))
        if ou < 6:
            return
        st = stage[oct_i % 2]
        OP("vector", lambda e, st=st: e.tensor_tensor(st[:, :], pY[:, 0:512],
           xsh_l0[:, oc:oc + 512], A.mult), reads=(pY.name, "xsh_l0"), writes=(st.name,))
        OP("scalar", lambda e, st=st: e.activation(dump[:, 0:512], st[:, :], AF.Square,
           accum_out=ssq8[:, p * 8 + oct_i:p * 8 + oct_i + 1]),
           reads=(st.name,), writes=("dump", f"ssq8_{p}"))
        DMA("gpsimd", lambda e, st=st: e.dma_start(yg_scr[g0:g0 + 128, oc:oc + 512],
            st[:, :]), reads=(st.name,), writes=(f"yg_{t}_{p}_{oct_i}",))

    def gen_istd(t, p):
        OP("vector", lambda e, p=p: e.reduce_sum(ssq_t[:], ssq8[:, p * 8:(p + 1) * 8],
           axis=AX), reads=(f"ssq8_{p}",), writes=("ssq_t",))
        istd_pipe("ssq_t", ssq_t[:], var_t[:], "var_t", 1, 1.0 / DIN)
        OP("vector", lambda e, p=p: e.tensor_copy(istd_t[:, p:p + 1], var_t[:]),
           reads=("var_t",), writes=("istd_t",))

    def gen_outproj_ffn(t, qtoks):
        trow = t * T
        yg_deps = [sch.last_w[f"yg_{t}_{p}_{o}"] for p in range(NL0) for o in range(8)]
        for lp in range(2):
            for k in range(32):
                import os
                _db = os.environ.get('ODB', '1') == '1'
                wg, lb = (wgt[k % 2], lhsb[k % 2]) if _db else (wgt[0], lhsb[0])
                DMA("sync", lambda e, k=k, wg=wg: e.dma_start(
                    wg[:, 0:D], wq_out[k * 128:(k + 1) * 128, 0:D]),
                    writes=(wg.name,), extra=[tt for (_, _, tt) in qtoks["wq_out"]])
                DMA("sync", lambda e, k=k, lb=lb, trow=trow: e.dma_start_transpose(
                    lb[:, 0:T], yg_scr[trow:trow + T, k * 128:(k + 1) * 128]),
                    writes=(lb.name,), extra=yg_deps)
                def mmo(e, k=k, lp=lp, wg=wg, lb=lb):
                    inst = None
                    npsw(e, 0)
                    for i in range(2):
                        for dc in range(4):
                            pp = P[i * 4 + dc] if i * 4 + dc < 7 else psm
                            inst = e.matmul(pp[:, 0:512],
                                            lb[:, (lp * 2 + i) * 128:(lp * 2 + i + 1) * 128],
                                            wg[:, dc * 512:(dc + 1) * 512],
                                            start=(k == 0), stop=(k == 31))
                    return inst
                OP("tensor", mmo, reads=(wg.name, lb.name),
                   writes=tuple(P[i].name for i in range(7)) + ("psm",))
            for i in range(2):
                p_ = lp * 2 + i
                g0 = trow + p_ * 128
                gen_istd(t, p_)
                buf = xl0 if i == 0 else scr2
                for dc in range(4):
                    pp = P[i * 4 + dc] if i * 4 + dc < 7 else psm
                    OP("vector", lambda e, dc=dc, pp=pp, b=buf, p_=p_:
                       e.tensor_scalar_mul(b[:, dc * 512:(dc + 1) * 512], pp[:, 0:512],
                                           istd_t[:, p_:p_ + 1]),
                       reads=(pp.name, "istd_t"), writes=(buf.name,))
                if "attnb" in taps and t == 0 and p_ == 0:
                    DMA("gpsimd", lambda e, b=buf: e.dma_start(taps["attnb"][:, :], b[:, 0:D]),
                        reads=(buf.name,), writes=("tapd",))
                if "istd4" in taps and t == 0 and p_ == 0:
                    OP("vector", lambda e: e.tensor_copy(dump[:, 0:8], ssq8[:, 0:8]),
                       reads=("ssq8",), writes=("dump",))
                    DMA("gpsimd", lambda e: e.dma_start(taps["istd4"][:, :], dump[:, 0:8]),
                        reads=("dump",), writes=("tapd",))
                ln_stats(buf[:, 0:D], buf.name)
                DMA("sync", lambda e, g0=g0: e.dma_start(scr3[:], x_in[g0:g0 + 128, :]),
                    writes=("qnat",))
                OP("vector", lambda e, b=buf: e.tensor_scalar(b[:, 0:D], b[:, 0:D],
                   mean_t[:], var_t[:], A.subtract, A.mult),
                   reads=(buf.name, "mean_t", "var_t"), writes=(buf.name,))
                OP("vector", lambda e, b=buf: e.tensor_tensor(b[:, 0:D], b[:, 0:D],
                   scr3[:, 0:D], A.add), reads=(buf.name, "qnat"), writes=(buf.name,))
                DMA("gpsimd", lambda e, g0=g0, b=buf: e.dma_start(
                    x2_scr[g0:g0 + 128, :], b[:, 0:D]), reads=(buf.name,),
                    writes=(f"x2_{t}_{p_}",))
                ln_stats(buf[:, 0:D], buf.name)
                OP("vector", lambda e, b=buf: e.tensor_scalar(dump[:, 0:D], b[:, 0:D],
                   mean_t[:], var_t[:], A.subtract, A.mult),
                   reads=(buf.name, "mean_t", "var_t"), writes=("dump",))
                OP("vector", lambda e: e.reduce_max(m2_t[:], dump[:, 0:D], axis=AX,
                   apply_absolute_value=True), reads=("dump",), writes=("m2_t",))
                OP("vector", lambda e: e.tensor_scalar(m2_t[:], m2_t[:], 1.0 / 127.0,
                   1e-12, A.mult, A.max), reads=("m2_t",), writes=("m2_t",))
                OP("vector", lambda e, p_=p_: e.tensor_copy(xs_t[:, p_:p_ + 1], m2_t[:]),
                   reads=("m2_t",), writes=("xs_t",))
                OP("vector", lambda e: e.reciprocal(var_t[:], m2_t[:]),
                   reads=("m2_t",), writes=("var_t",))
                OP("vector", lambda e: e.tensor_scalar(qu[:, 0:D], dump[:, 0:D],
                   var_t[:], 128.5, A.mult, A.add), reads=("dump", "var_t"), writes=("qu",))
                OP("vector", lambda e: e.tensor_scalar(qk[:, 0:D], qu[:, 0:D], -0.5,
                   None, A.add, A.bypass), reads=("qu",), writes=("qk",))
                OP("vector", lambda e: e.scalar_tensor_tensor(qu[:, 0:D], qu[:, 0:D],
                   -1.0, qk[:, 0:D], A.add, A.is_ge), reads=("qu", "qk"), writes=("qu",))
                OP("vector", lambda e: e.tensor_tensor(qu[:, 0:D], qu[:, 0:D],
                   qk[:, 0:D], A.add), reads=("qu", "qk"), writes=("qu",))
                OP("vector", lambda e: e.tensor_scalar(qfq[:, 0:D], qu[:, 0:D],
                   -128.0, None, A.add, A.bypass), reads=("qu",), writes=("qfq",))
                DMA("gpsimd", lambda e, g0=g0: e.dma_start(xq_scr[g0:g0 + 128, :],
                    qfq[:, 0:D]), reads=("qfq",), writes=(f"xq_{t}_{p_}",))
        for k in range(16):
            DMA("sync", lambda e, k=k, trow=trow: e.dma_start_transpose(
                xnT[:, k * T:(k + 1) * T], xq_scr[trow:trow + T, k * 128:(k + 1) * 128]),
                writes=("xnT",),
                extra=[sch.last_w[f"xq_{t}_{p}"] for p in range(NL0)])
        for lp in range(2):
            for k in range(16):
                wg = wgt[k % 2]
                DMA("sync", lambda e, k=k, wg=wg: e.dma_start(
                    wg[:, 0:D], wq_ffn[k * 128:(k + 1) * 128, 0:D]),
                    writes=(wg.name,), extra=[tt for (_, _, tt) in qtoks["wq_ffn"]])
                def mmf(e, k=k, lp=lp, wg=wg):
                    inst = None
                    npsw(e, 0)
                    for i in range(2):
                        for oc in range(4):
                            pp = P[i * 4 + oc] if i * 4 + oc < 7 else psm
                            inst = e.matmul(pp[:, 0:512],
                                            xnT[:, k * T + (lp * 2 + i) * 128:k * T + (lp * 2 + i + 1) * 128],
                                            wg[:, oc * 512:(oc + 1) * 512],
                                            start=(k == 0), stop=(k == 15))
                    return inst
                OP("tensor", mmf, reads=(wg.name, "xnT"),
                   writes=tuple(P[i].name for i in range(7)) + ("psm",))
            for i in range(2):
                p_ = lp * 2 + i
                g0 = trow + p_ * 128
                buf = xl0 if i == 0 else scr2
                for oc in range(4):
                    pp = P[i * 4 + oc] if i * 4 + oc < 7 else psm
                    OP("vector", lambda e, oc=oc, pp=pp, b=buf, p_=p_:
                       e.scalar_tensor_tensor(b[:, oc * 512:(oc + 1) * 512], pp[:, 0:512],
                                              xs_t[:, p_:p_ + 1],
                                              ws_bc[:, oc * 512:(oc + 1) * 512],
                                              A.mult, A.mult),
                       reads=(pp.name, "xs_t", "ws_bc"), writes=(buf.name,))
                OP("vector", lambda e, b=buf: e.tensor_tensor(b[:, 0:D], b[:, 0:D],
                   bias_bc[:, 0:D], A.add), reads=(buf.name, "bias_bc"), writes=(buf.name,))
                ln_stats(buf[:, 0:D], buf.name)
                DMA("sync", lambda e, g0=g0: e.dma_start(scr3[:], x2_scr[g0:g0 + 128, :]),
                    writes=("qnat",), extra=[sch.last_w[f"x2_{t}_{p_}"]])
                OP("vector", lambda e, b=buf: e.tensor_scalar(b[:, 0:D], b[:, 0:D],
                   mean_t[:], var_t[:], A.subtract, A.mult),
                   reads=(buf.name, "mean_t", "var_t"), writes=(buf.name,))
                OP("vector", lambda e, b=buf: e.tensor_tensor(b[:, 0:D], b[:, 0:D],
                   scr3[:, 0:D], A.add), reads=(buf.name, "qnat"), writes=(buf.name,))
                DMA("gpsimd", lambda e, g0=g0, b=buf: e.dma_start(
                    y_out[g0:g0 + 128, :], b[:, 0:D]), reads=(buf.name,),
                    writes=(f"y_{t}_{p_}",))

    qtoks = {"wq_in": [], "wq_out": [], "wq_ffn": []}
    gen_quant_win(win_jobs(0, 4384), qtoks)
    gen_ln_x(0)
    gen_xnT(0)
    gen_inproj(0, qtoks, fcs=set(range(0, 9)))
    gen_quant_win(win_jobs(4384, NPROJ), qtoks)
    gen_quant_wout(qtoks)
    gen_inproj(0, qtoks, fcs=set(range(9, 17)))
    for p in range(NL0):
        gen_smalls(0, p)
        gen_ssd(0, p, qtoks, ssd_upto)
    gen_quant_ffn(qtoks)
    gen_outproj_ffn(0, qtoks)
    for t in range(1, NT):
        gen_ln_x(t)
        gen_xnT(t)
        gen_inproj(t, qtoks)
        for p in range(NL0):
            gen_smalls(t, p)
            gen_ssd(t, p, qtoks, ssd_upto)
        gen_outproj_ffn(t, qtoks)
    OP("sync", lambda e: e.nop(),
       extra=[sch.last_w[f"y_{t}_{p}"] for t in range(NT) for p in range(NL0)])
    with nc.Block() as block:
        sch.emit(block)
    print("SBUF remaining:", nc.sbuf_bytes_remaining, "PSUM banks remaining:", nc.psum_banks_remaining)
    ninstr = {e: len(sch.ops[e]) for e in sch.ENGINES}
    print("ops per engine:", ninstr, "total", sum(ninstr.values()))
    es.close()
    return nc



_NC_CACHE = [None]

def kernel(L0_attn_x, L0_attn_W_in, L0_attn_W_theta, L0_attn_W_lambda, L0_attn_W_out,
           L0_attn_B_bias, L0_attn_C_bias, L0_attn_A_log, L0_attn_D_param,
           L0_attn_dt_bias, L0_ffn_weight, L0_ffn_bias):
    from concourse.bass_utils import run_bass_kernel_spmd
    if _NC_CACHE[0] is None:
        _NC_CACHE[0] = build()
    nc = _NC_CACHE[0]
    x = np.ascontiguousarray(np.asarray(L0_attn_x, np.float32))
    shared = {
        "w_in": np.ascontiguousarray(np.asarray(L0_attn_W_in, np.float32)),
        "w_theta": np.ascontiguousarray(np.asarray(L0_attn_W_theta, np.float32)),
        "w_lambda": np.ascontiguousarray(np.asarray(L0_attn_W_lambda, np.float32)),
        "w_out": np.ascontiguousarray(np.asarray(L0_attn_W_out, np.float32)),
        "b_bias": np.asarray(L0_attn_B_bias, np.float32).reshape(1, NST),
        "c_bias": np.asarray(L0_attn_C_bias, np.float32).reshape(1, NST),
        "a_log": np.asarray(L0_attn_A_log, np.float32).reshape(1, NH),
        "d_param": np.asarray(L0_attn_D_param, np.float32).reshape(1, NH),
        "dt_bias": np.asarray(L0_attn_dt_bias, np.float32).reshape(1, NH),
        "ffn_w": np.ascontiguousarray(np.asarray(L0_ffn_weight, np.float32)),
        "ffn_b": np.asarray(L0_ffn_bias, np.float32).reshape(1, D),
    }
    in_maps = [dict(shared, x=x[i]) for i in range(8)]
    res = run_bass_kernel_spmd(nc, in_maps, core_ids=list(range(8)))
    return np.stack([res.results[i]["y"] for i in range(8)], axis=0)
